# version 27
# baseline (speedup 1.0000x reference)
import numpy as np
import ml_dtypes
from contextlib import ExitStack

import concourse.bass as bass
import concourse.mybir as mybir
from concourse.bass_utils import run_bass_kernel_spmd

F32 = mybir.dt.float32
BF16 = mybir.dt.bfloat16
AF = mybir.ActivationFunctionType
ALU = mybir.AluOpType
AX = mybir.AxisListType

NCORES = 8
S, D = 8192, 2048
T = S // NCORES
NT = T // 128
KD = D // 128
H, DK, DV = 16, 128, 128
SEG = 512
NSEG = S // SEG
E, FF = 8, 1024
FC = FF // 128
SCALE = 1.0 / float(DK) ** 0.5
SHIFT = -16.0
LNEPS = 1e-5
RMSEPS = 1e-6
NROUND = NSEG - 2
DEV_FFN_STAGE = "full"
DEV_NEXP = None
DEV_DUMP = False

P0 = 17408
AFB = 29952
A1 = 62720
A2 = 95488
A3 = 128256
RR = 161024


class B:
    def __init__(self):
        self.nc = bass.Bass("TRN2", target_bir_lowering=False,
                            detect_race_conditions=False)
        self.es = ExitStack()
        self.sem_h, self.sem_n, self.last_wait = {}, {}, {}

    def sems(self, names):
        for n in names:
            self.sem_h[n] = self.es.enter_context(self.nc.semaphore(n))
            self.sem_n[n] = 0

    def inc(self, instr, name, k=1):
        instr.then_inc(self.sem_h[name], k)
        self.sem_n[name] += k
        return (name, self.sem_n[name])

    def cur(self, name):
        return (name, self.sem_n[name])

    def wait(self, eng, ename, tok):
        if tok is None:
            return
        sname, val = tok
        if val <= 0:
            return
        key = (ename, sname)
        if self.last_wait.get(key, -1) >= val:
            return
        eng.wait_ge(self.sem_h[sname], val)
        self.last_wait[key] = val

    def chain(self, eng, ename, dom, ins):
        self.wait(eng, ename, self.inc(ins, dom))

    def sb(self, name, shape, dtype, off):
        return self.nc.alloc_sbuf_tensor_at(name, shape, dtype, offset=off)

    def build(self, parts=("ffn1", "attn", "ffn2")):
        nc = self.nc
        d = lambda n, s, dt, k: nc.dram_tensor(n, s, dt, kind=k).ap()
        self.x_d = d("x", [T, D], F32, "ExternalInput")
        self.wq_d = d("wq", [128, 4 * KD * 512], F32, "ExternalInput")
        self.wk_d = d("wk", [128, 4 * KD * 512], F32, "ExternalInput")
        self.wv_d = d("wv", [128, 4 * KD * 512], F32, "ExternalInput")
        self.wo_d = d("wo", [128, KD * D], F32, "ExternalInput")
        self.betas_d = d("betas", [H, DV], F32, "ExternalInput")
        self.rmsw_d = d("rmsw", [128, KD], F32, "ExternalInput")
        self.router_d = d("router", [128, KD * E], F32, "ExternalInput")
        self.wg_d = d("wg", [E, 128, 2 * KD * 512], F32, "ExternalInput")
        self.wu_d = d("wu", [E, 128, 2 * KD * 512], F32, "ExternalInput")
        self.wd_d = d("wd", [E, 128, FC * D], F32, "ExternalInput")
        self.swg_d = d("swg", [128, 2 * KD * 512], F32, "ExternalInput")
        self.swu_d = d("swu", [128, 2 * KD * 512], F32, "ExternalInput")
        self.swd_d = d("swd", [128, FC * D], F32, "ExternalInput")
        self.eyeb_d = d("eyeb", [128, 128], BF16, "ExternalInput")
        self.eyef_d = d("eyef", [128, 128], F32, "ExternalInput")
        self.o128_d = d("o128", [128, 1], BF16, "ExternalInput")
        self.o1_d = d("o1", [1, 128], BF16, "ExternalInput")
        self.sel8_d = d("sel8", [8, 8 * 128], BF16, "ExternalInput")
        self.shiftc_d = d("shiftc", [128, 1], F32, "ExternalInput")
        self.maskt_d = d("maskt", [128, 4 * SEG], BF16, "ExternalInput")
        self.zmask_d = d("zmaskc", [128, 2 * NSEG], F32, "ExternalInput")
        self.rmask_d = d("rmaskc", [128, 2 * NROUND], F32, "ExternalInput")
        self.out_d = d("out", [T, D], F32, "ExternalOutput")
        if DEV_DUMP:
            self.dbg_ht = d("dbg_ht", [128, KD * T], F32, "ExternalOutput")
            self.dbg_actp = d("dbg_actp", [128, FC * T], F32, "ExternalOutput")
            self.dbg_sg = d("dbg_sg", [128, FC * SEG], F32, "ExternalOutput")
            self.dbg_wbc = d("dbg_wbc", [128, T], F32, "ExternalOutput")
            self.dbg_wt = d("dbg_wt", [8, T], F32, "ExternalOutput")
            self.dbg_httm = d("dbg_httm", [128, D], F32, "ExternalOutput")
            self.dbg_lnx = d("dbg_lnx", [128, D], F32, "ExternalOutput")
            self.dbg_st8 = d("dbg_st8", [128, 64], F32, "ExternalOutput")
        self.rsp_d = self.nc.dram_tensor("rspill", [T, D], F32).ap()
        self.wqb_d = self.nc.dram_tensor("wqb", [128, 4 * KD * 512], BF16).ap()
        self.wkb_d = self.nc.dram_tensor("wkb", [128, 4 * KD * 512], BF16).ap()
        self.wvb_d = self.nc.dram_tensor("wvb", [128, 4 * KD * 512], BF16).ap()
        self.cc1i_t = nc.dram_tensor("cc1i", [128, 2 * H], F32)
        self.cc1o_t = nc.dram_tensor("cc1o", [NCORES * 128, 2 * H], F32)
        self.cc2i_t = nc.dram_tensor("cc2i", [128, H * SEG], BF16)
        self.cc2o_t = nc.dram_tensor("cc2o", [NCORES * 128, H * SEG], BF16)

        o = P0
        def al(name, shape, dtype, nbytes):
            nonlocal o
            t = self.sb(name, shape, dtype, o)
            o += (nbytes + 63) & ~63
            return t
        self.eyeb = al("eyeb_s", [128, 128], BF16, 256)
        self.eyef = al("eyef_s", [128, 128], F32, 512)
        self.o128 = al("o128_s", [128, 1], BF16, 32)
        self.o1 = al("o1_s", [1, 128], BF16, 256)
        self.sel8 = al("sel8_s", [8, 8 * 128], BF16, 2048)
        self.routf = al("routf", [128, KD * E], BF16, 256)
        self.rmsc = al("rmsc", [128, KD], F32, 64)
        self.beta = al("beta_s", [128, H], F32, 64)
        betin_off = o
        self.betin = al("betin", [16, 128], F32, 512)
        self.beta1m = self.sb("beta1m", [128, H], F32, betin_off)
        self.logit = al("logit", [8, T], F32, 4096)
        self.rsc = al("rsc", [128, 40], F32, 160)
        self.wde = al("wde", [128, NT * E], F32, 256)
        self.wde16 = al("wde16", [128, NT * E], BF16, 128)
        self.wT = al("wT", [8, T], BF16, 2048)
        self.zsb = al("zsb", [128, NCORES * 2 * H], F32, 1024)
        self.z_a = al("z_a", [128, H], F32, 64)
        self.z_b = al("z_b", [128, H], F32, 64)
        self.z16 = al("z16", [128, 2 * H], BF16, 64)
        self.zmask = al("zmask", [128, 2 * NSEG], F32, 128)
        self.rmask = al("rmask", [128, 2 * NROUND], F32, 128)
        self.st8 = al("st8", [128, 64], F32, 256)
        self.shiftc = al("shiftc", [128, 1], F32, 32)
        assert o <= AFB, o

        sb = self.sb
        self.act_fm = sb("act_fm", [128, KD * T], BF16, AFB)
        self.att_all = sb("att_all", [128, KD * T], BF16, AFB)
        self.wcur = sb("wcur", [128, KD * SEG], BF16, A1)
        self.wnext = sb("wnext", [128, KD * SEG], BF16, A1 + 16384)
        self.q_fm = sb("q_fm", [128, KD * T], BF16, A1)
        self.wdn = sb("wdn", [128, FC * D], BF16, A2)
        self.k_fm = sb("k_fm", [128, KD * T], BF16, A2)
        self.wo_res = sb("wo_res", [128, KD * D], BF16, A1)
        self.httm2 = sb("httm2", [128, D], BF16, A3)
        self.lnq = sb("lnq", [128, D], F32, A3 + 8192)
        self.actp = sb("actp", [128, FC * T], BF16, A3)
        self.sg = sb("sg", [128, FC * SEG], BF16, A3 + 16384)
        self.wbc = sb("wbc", [128, T], F32, A3 + 24576)
        self.wbc2 = sb("wbc2", [128, T], F32, A3 + 28672)
        self.httm = sb("httm", [128, D], BF16, A3 + 28672)
        self.v_tm = sb("v_tm", [128, KD * T], BF16, A3)
        self.r = sb("r_s", [128, NT * D], F32, RR)
        self.wqkv = [sb("wqkv0", [128, KD * SEG], BF16, RR),
                     sb("wqkv1", [128, KD * SEG], BF16, RR + 16384)]
        self.maskt = sb("maskt_s", [128, 4 * SEG], BF16, RR)
        self.keep2 = sb("keep2", [128, H * 128], BF16, RR + 32768)
        self.mem16b = sb("mem16b", [128, H * 128], BF16, RR + 36864)
        self.slab = [sb("slab0", [128, H * 256], BF16, RR + 8192),
                     sb("slab1", [128, H * 256], BF16, RR + 16384)]
        self.mem16 = sb("mem16", [128, H * 128], BF16, RR + 40960)
        self.fsc = sb("fsc", [128, 4 * SEG], F32, RR + 45056)
        self.rdc = sb("rdc", [128, 16], F32, RR + 64576)
        self.expT = sb("expT", [128, 4 * SEG], BF16, RR + 53312)
        self.expT2 = sb("expT2", [128, 4 * SEG], BF16, RR + 57408)
        self.attTs = [sb("attTs0", [128, SEG], BF16, RR + 61504),
                      sb("attTs1", [128, SEG], BF16, RR + 62528)]
        self.attC = sb("attC", [128, SEG], BF16, RR + 63552)
        self.skf2 = [sb("skf2a", [128, 1024], BF16, RR + 32768),
                     sb("skf2b", [128, 1024], BF16, RR + 34816)]
        self.skt2 = [sb("skt2a", [128, 1024], BF16, RR + 36864),
                     sb("skt2b", [128, 1024], BF16, RR + 38912)]
        self.sks2 = [sb("sks2a", [128, 1024], BF16, RR + 40960),
                     sb("sks2b", [128, 1024], BF16, RR + 43008)]
        self.gstg = [sb("gstga", [128, 512], BF16, RR + 57408),
                     sb("gstgb", [128, 512], BF16, RR + 58432)]

        self.sems(["io", "sc", "wg", "cc", "pe", "dv", "ac", "st",
                   "s0", "s1", "s2", "s3", "g0", "g1", "w0", "w1", "w2", "w3"])
        self.sg_war = None
        self.down_war = None
        self.qkv_fill = None
        self.ckq, self.ckv = {}, {}
        self.pM_free = None
        self.cpt_free = None
        self.gbst_war = None

        with nc.Block() as block:
            self.blk = block
            self.emit_prelude()
            if "ffn1" in parts:
                self.emit_ffn(0)
            if "attn" in parts:
                self.emit_attn()
            if "ffn2" in parts:
                self.emit_ffn(1)
            self.emit_store()
        self.es.close()
        return nc

    def emit_prelude(self):
        b, blk = self, self.blk

        @blk.sync
        def _(sy):
            for i in range(NT):
                b.inc(sy.dma_start(out=b.r[:, i * D:(i + 1) * D],
                                   in_=b.x_d[i * 128:(i + 1) * 128, :]), "io", 16)
            for dst, src in [(b.eyeb, b.eyeb_d), (b.eyef, b.eyef_d),
                             (b.o128, b.o128_d), (b.o1, b.o1_d),
                             (b.sel8, b.sel8_d), (b.shiftc, b.shiftc_d),
                             (b.betin, b.betas_d), (b.zmask, b.zmask_d),
                             (b.rmask, b.rmask_d)]:
                b.inc(sy.dma_start(out=dst[:, :], in_=src[:, :]), "io", 16)
            b.inc(sy.dma_start(out=b.rmsc[:, :], in_=b.rmsw_d[:, :]),
                  "io", 16)

        @blk.gpsimd
        def _(g):
            b.inc(g.dma_start(out=b.routf[:, :], in_=b.router_d[:, :]),
                  "wg", 16)
        self.prelude_io = self.cur("io")
        self.prelude_wg = self.cur("wg")

    def emit_norm(self, dst_fm, rms, pt):
        b, blk = self, self.blk
        bar_dv = b.cur("dv")
        bar_ac = b.cur("ac")
        bar_pe = b.cur("pe")
        st = b.st8

        @blk.vector
        def _(v):
            b.wait(v, "vector", b.prelude_io)
            b.wait(v, "vector", bar_ac)
            for i in range(NT):
                ins = v.tensor_reduce(st[:, i:i + 1], b.r[:, i * D:(i + 1) * D],
                                      axis=AX.X, op=ALU.add, negate=True)
            b.inc(ins, "dv")
        red_tok = b.cur("dv")

        @blk.scalar
        def _(sc):
            b.wait(sc, "scalar", b.prelude_io)
            b.wait(sc, "scalar", bar_dv)
            b.wait(sc, "scalar", bar_pe)
            for i in range(NT):
                ins = sc.activation(b.lnq[:, :], b.r[:, i * D:(i + 1) * D],
                                    AF.Square, accum_out=st[:, 16 + i:17 + i])
            b.inc(ins, "ac")
        sq_tok = b.cur("ac")

        c0b, c2b = st[:, 0:8], st[:, 16:24]
        c1b, c3b, c4b, c5b, c6b, c7b = (st[:, 8:16], st[:, 24:32],
                                        st[:, 32:40], st[:, 40:48],
                                        st[:, 48:56], st[:, 56:64])

        @blk.vector
        def _(v):
            b.wait(v, "vector", red_tok)
            b.wait(v, "vector", sq_tok)
            b.chain(v, "vector", "dv",
                    v.tensor_scalar(c1b, c0b, 1.0 / D, None, op0=ALU.mult))
            b.chain(v, "vector", "dv",
                    v.tensor_tensor(c3b, c1b, c1b, op=ALU.mult))
            b.chain(v, "vector", "dv",
                    v.tensor_scalar(c4b, c2b, 1.0 / D, None, op0=ALU.mult))
            b.chain(v, "vector", "dv",
                    v.tensor_tensor(c4b, c4b, c3b, op=ALU.subtract))
            b.inc(v.tensor_scalar(c5b, c4b, LNEPS, None, op0=ALU.add), "dv")

        @blk.scalar
        def _(sc):
            b.wait(sc, "scalar", b.cur("dv"))
            b.inc(sc.activation(c5b, c5b, AF.Sqrt), "ac")

        @blk.vector
        def _(v, rms=rms):
            b.wait(v, "vector", b.cur("ac"))
            b.chain(v, "vector", "dv", v.reciprocal(c5b, c5b))
            if rms:
                b.chain(v, "vector", "dv",
                        v.tensor_tensor(c6b, c5b, c5b, op=ALU.mult))
                b.chain(v, "vector", "dv",
                        v.tensor_tensor(c6b, c6b, c4b, op=ALU.mult))
                b.inc(v.tensor_scalar(c6b, c6b, RMSEPS, None, op0=ALU.add),
                      "dv")

        if rms:
            @blk.scalar
            def _(sc):
                b.wait(sc, "scalar", b.cur("dv"))
                b.inc(sc.activation(c6b, c6b, AF.Sqrt), "ac")

            @blk.vector
            def _(v):
                b.wait(v, "vector", b.cur("ac"))
                b.chain(v, "vector", "dv", v.reciprocal(c6b, c6b))
                b.chain(v, "vector", "dv",
                        v.tensor_tensor(c5b, c5b, c6b, op=ALU.mult))

        @blk.vector
        def _(v):
            b.chain(v, "vector", "dv",
                    v.tensor_tensor(c3b, c1b, c5b, op=ALU.mult))
        stats_tok = b.cur("dv")

        htb = [b.httm, b.httm2]
        ht_done = {}
        for i in range(NT):
            @blk.vector
            def _(v, i=i):
                b.wait(v, "vector", stats_tok)
                if i >= 2:
                    b.wait(v, "vector", ht_done[i - 2])
                if i == 0:
                    b.wait(v, "vector", bar_pe)
                b.inc(v.tensor_scalar(htb[i % 2][:, :], b.r[:, i * D:(i + 1) * D],
                                      c5b[:, i:i + 1], c3b[:, i:i + 1],
                                      op0=ALU.mult, op1=ALU.add), "dv")
            hdv = b.cur("dv")

            @blk.tensor
            def _(t, i=i, hdv=hdv):
                b.wait(t, "tensor", hdv)
                for kc in range(KD):
                    pp = pt[kc % 2]
                    b.wait(t, "tensor", self.pt_free.get(kc % 2))
                    self.pt_ready[kc % 2] = b.inc(
                        t.transpose(pp[:, 0:128],
                                    htb[i % 2][:, kc * 128:(kc + 1) * 128],
                                    b.eyeb[:, :]), "pe")

                    @blk.vector
                    def _(v, i=i, kc=kc, pp=pp):
                        b.wait(v, "vector", self.pt_ready[kc % 2])
                        col = kc * T + i * 128
                        if rms:
                            ins = v.tensor_scalar(dst_fm[:, col:col + 128],
                                                  pp[:, 0:128],
                                                  b.rmsc[:, kc:kc + 1], None,
                                                  op0=ALU.mult)
                        else:
                            ins = v.tensor_copy(dst_fm[:, col:col + 128],
                                                pp[:, 0:128])
                        self.pt_free[kc % 2] = b.inc(ins, "dv")
            ht_done[i] = self.cur("pe")
        return b.cur("dv")

    def emit_ffn(self, call):
        b, blk, nc = self, self.blk, self.nc
        pes = ExitStack()
        pt = [pes.enter_context(nc.psum_tensor(f"pt{call}_{j}", [128, 1024], BF16))
              for j in range(2)]
        pl = [pes.enter_context(nc.psum_tensor(f"pl{call}_{j}", [128, 512], F32))
              for j in range(2)]
        self.pt_ready, self.pt_free = {}, {}

        ht_tok = self.emit_norm(b.act_fm, rms=True, pt=pt)
        if DEV_FFN_STAGE == "norm":
            if DEV_DUMP:
                self.emit_dump()
            pes.close()
            return

        @blk.tensor
        def _(t):
            b.wait(t, "tensor", ht_tok)
            b.wait(t, "tensor", b.prelude_wg)
            b.wait(t, "tensor", b.cur("dv"))
            for tt in range(2):
                for kk in range(KD):
                    ins = t.matmul(pl[tt][0:8, :], b.routf[:, kk * E:(kk + 1) * E],
                             b.act_fm[:, kk * T + tt * 512: kk * T + (tt + 1) * 512],
                             start=(kk == 0), stop=(kk == KD - 1))
            b.inc(ins, "pe")
        lg_pe = b.cur("pe")

        @blk.vector
        def _(v):
            b.wait(v, "vector", lg_pe)
            v.tensor_copy(b.logit[:, 0:512], pl[0][0:8, :])
            b.inc(v.tensor_copy(b.logit[:, 512:1024], pl[1][0:8, :]), "dv")
        lg_dv = b.cur("dv")
        if DEV_FFN_STAGE == "logits":
            pes.close()
            return
        tp_free = {0: None, 1: None}

        for sl in range(NT):
            @blk.tensor
            def _(t, sl=sl):
                b.wait(t, "tensor", lg_dv)
                b.wait(t, "tensor", tp_free[sl % 2])
                b.inc(t.transpose(pl[sl % 2][:, 0:8],
                                  b.logit[:, sl * 128:(sl + 1) * 128],
                                  b.eyef[0:8, 0:8]), "pe")
            tpr = b.cur("pe")

            @blk.vector
            def _(v, sl=sl, tpr=tpr):
                b.wait(v, "vector", tpr)
                lt, pr = b.rsc[:, 0:8], b.rsc[:, 8:16]
                m = b.st8[:, 56:57]
                v.tensor_copy(lt, pl[sl % 2][:, 0:8])
                tok = b.inc(v.tensor_reduce(
                    m, lt, axis=AX.X, op=ALU.max, negate=True), "dv")
                tp_free[sl % 2] = tok
                b.wait(v, "vector", tok)
                b.chain(v, "vector", "dv",
                        v.tensor_scalar(pr, lt, m, None, op0=ALU.add))

            @blk.scalar
            def _(sc):
                b.wait(sc, "scalar", b.cur("dv"))
                b.inc(sc.activation(b.rsc[:, 8:16], b.rsc[:, 8:16], AF.Exp,
                                    accum_out=b.st8[:, 57:58]), "ac")

            if DEV_FFN_STAGE == "sfm1":
                continue

            @blk.vector
            def _(v, sl=sl):
                b.wait(v, "vector", b.cur("ac"))
                pr, t1, t2 = b.rsc[:, 8:16], b.rsc[:, 16:24], b.rsc[:, 24:32]
                m = b.st8[:, 58:59]
                b.chain(v, "vector", "dv", v.reciprocal(m, b.st8[:, 57:58]))
                b.chain(v, "vector", "dv",
                        v.tensor_scalar(pr, pr, m, None, op0=ALU.mult))
                b.chain(v, "vector", "dv",
                        v.tensor_reduce(m, pr, axis=AX.X, op=ALU.max))
                b.chain(v, "vector", "dv",
                        v.tensor_scalar(t1, pr, m, None, op0=ALU.is_ge))
                b.chain(v, "vector", "dv",
                        v.tensor_tensor(t2, pr, t1, op=ALU.mult))
                b.chain(v, "vector", "dv",
                        v.tensor_tensor(t2, pr, t2, op=ALU.subtract))
                b.chain(v, "vector", "dv",
                        v.tensor_reduce(m, t2, axis=AX.X, op=ALU.max))
                b.chain(v, "vector", "dv",
                        v.tensor_scalar(t1, pr, m, None, op0=ALU.is_ge))
                b.chain(v, "vector", "dv",
                        v.tensor_tensor(t2, pr, t1, op=ALU.mult))
                b.chain(v, "vector", "dv",
                        v.tensor_reduce(m, t2, axis=AX.X, op=ALU.add))
                b.chain(v, "vector", "dv", v.reciprocal(m, m))
                wsl = b.wde[:, sl * E:(sl + 1) * E]
                b.chain(v, "vector", "dv",
                        v.tensor_scalar(wsl, t2, m, None, op0=ALU.mult))
                b.inc(v.tensor_copy(b.wde16[:, sl * E:(sl + 1) * E], wsl), "dv")

            @blk.tensor
            def _(t, sl=sl):
                b.wait(t, "tensor", b.cur("dv"))
                b.wait(t, "tensor", self.pt_free.get(0))
                b.inc(t.transpose(pt[0][0:8, 0:128],
                                  b.wde16[:, sl * E:(sl + 1) * E],
                                  b.eyeb[:, :]), "pe")

            @blk.vector
            def _(v, sl=sl):
                b.wait(v, "vector", b.cur("pe"))
                self.pt_free[0] = b.inc(
                    v.tensor_copy(b.wT[:, sl * 128:(sl + 1) * 128],
                                  pt[0][0:8, 0:128]), "dv")
        wT_tok = b.cur("dv")
        pes.close()
        if DEV_FFN_STAGE in ("router", "sfm1"):
            if DEV_DUMP:
                self.emit_dump()
            return

        pes2 = ExitStack()
        pb = [pes2.enter_context(nc.psum_tensor(f"pb{call}_{j}", [128, 512], F32))
              for j in range(8)]
        pb_free = {j: None for j in range(8)}

        slot_t = [b.wcur, b.wnext]
        slot_sem = ["s0", "s1"]
        slot_cons = [b.cur("pe"), b.cur("pe")]
        fills = []
        dwn_fills = []
        self.wdn_cons = b.cur("pe")

        def stream_half(g, idx, src2d):
            slot = idx % 2
            b.wait(g, "gpsimd", slot_cons[slot])
            tok = b.inc(g.dma_start(out=slot_t[slot][:, :], in_=src2d),
                        slot_sem[slot], 16)
            fills.append(tok)

        def half_src(w2d, half):
            return w2d[:, half * (KD * 512):(half + 1) * (KD * 512)]

        exps = [(self.wg_d[e], self.wu_d[e], self.wd_d[e], False)
                for e in range(E)] + [(self.swg_d, self.swu_d, self.swd_d, True)]

        if DEV_NEXP is not None:
            exps = exps[:DEV_NEXP]
        halves = []
        for (gd, ud, dd, sh) in exps:
            halves += [half_src(gd, 0), half_src(ud, 0),
                       half_src(gd, 1), half_src(ud, 1)]

        def wdown_stream(g, e):
            b.wait(g, "gpsimd", self.wdn_cons)
            tok = b.inc(g.dma_start(out=b.wdn[:, :], in_=exps[e][2][:, :]),
                        "s2", 16)
            dwn_fills.append(tok)

        @blk.gpsimd
        def _(g):
            stream_half(g, 0, halves[0])
            stream_half(g, 1, halves[1])
            wdown_stream(g, 0)
        hidx = 2

        wbc_buf = [b.wbc, b.wbc2]
        sg_war = {j: None for j in range(8)}
        sgrdy = {j: None for j in range(8)}

        def emit_wbc(e):
            @blk.tensor
            def _(t, e=e):
                b.wait(t, "tensor", wT_tok)
                b.wait(t, "tensor", pb_free[6])
                b.wait(t, "tensor", pb_free[7])
                t.matmul(pb[6][:, :], b.sel8[:, e * 128:(e + 1) * 128],
                         b.wT[0:8, 0:512], start=True, stop=True)
                ins = t.matmul(pb[7][:, :], b.sel8[:, e * 128:(e + 1) * 128],
                               b.wT[0:8, 512:1024], start=True, stop=True)
                b.inc(ins, "pe")

            @blk.vector
            def _(v, e=e):
                b.wait(v, "vector", b.cur("pe"))
                v.tensor_copy(wbc_buf[e % 2][:, 0:512], pb[6][:, :])
                tok = b.inc(v.tensor_copy(wbc_buf[e % 2][:, 512:1024],
                                          pb[7][:, :]), "dv")
                pb_free[6] = tok
                pb_free[7] = tok

        emit_wbc(0)
        for ei, (gd, ud, dd, sh) in enumerate(exps):
            for hp in range(4):
                gidx = ei * 4 + hp
                isg = (hp % 2 == 0)
                half = hp // 2
                slot = gidx % 2
                fill_tok = fills[gidx]
                for f in range(4):
                    for tt in range(2):
                        j = f * 2 + tt

                        @blk.tensor
                        def _(t, f=f, tt=tt, j=j, slot=slot, fill_tok=fill_tok):
                            b.wait(t, "tensor", fill_tok)
                            b.wait(t, "tensor", ht_tok)
                            b.wait(t, "tensor", pb_free[j])
                            sl = slot_t[slot]
                            for kk in range(KD):
                                ins = t.matmul(
                                    pb[j][:, :],
                                    sl[:, kk * 512 + f * 128: kk * 512 + (f + 1) * 128],
                                    b.act_fm[:, kk * T + tt * 512: kk * T + (tt + 1) * 512],
                                    start=(kk == 0), stop=(kk == KD - 1))
                            b.inc(ins, "pe")
                        bank_pe = b.cur("pe")
                        if isg:
                            @blk.scalar
                            def _(sc, j=j, bank_pe=bank_pe):
                                b.wait(sc, "scalar", bank_pe)
                                b.wait(sc, "scalar", sg_war[j])
                                b.inc(sc.activation(b.sg[:, j * 512:(j + 1) * 512],
                                                    pb[j][:, :], AF.Silu), "ac")
                            sgtok = b.cur("ac")
                            pb_free[j] = sgtok
                            if not sh:
                                @blk.vector
                                def _(v, j=j, tt=tt, ei=ei, sgtok=sgtok):
                                    b.wait(v, "vector", sgtok)
                                    b.inc(v.tensor_tensor(
                                        b.sg[:, j * 512:(j + 1) * 512],
                                        b.sg[:, j * 512:(j + 1) * 512],
                                        wbc_buf[ei % 2][:, tt * 512:(tt + 1) * 512],
                                        op=ALU.mult), "dv")
                                sgrdy[j] = b.cur("dv")
                            else:
                                sgrdy[j] = sgtok
                        else:
                            @blk.vector
                            def _(v, j=j, f=f, tt=tt, half=half, bank_pe=bank_pe):
                                b.wait(v, "vector", bank_pe)
                                b.wait(v, "vector", self.down_war)
                                b.wait(v, "vector", sgrdy[j])
                                col = (half * 4 + f) * T + tt * 512
                                ins = v.tensor_tensor(
                                    b.actp[:, col:col + 512],
                                    b.sg[:, j * 512:(j + 1) * 512],
                                    pb[j][:, :], op=ALU.mult)
                                tok = b.inc(ins, "dv")
                                pb_free[j] = tok
                                sg_war[j] = tok

                slot_cons[slot] = b.cur("pe")
                if hidx < len(halves):
                    @blk.gpsimd
                    def _(g, hidx=hidx):
                        stream_half(g, hidx, halves[hidx])
                    hidx += 1
                if hp == 1 and ei + 1 < len(exps) and not exps[ei + 1][3]:
                    emit_wbc(ei + 1)

            act_tok = b.cur("dv")

            @blk.tensor
            def _(t, ei=ei, act_tok=act_tok):
                b.wait(t, "tensor", dwn_fills[ei])
                b.wait(t, "tensor", act_tok)
                for ts in range(NT):
                    grp = 4 * (ts % 2)
                    for dsl in range(4):
                        b.wait(t, "tensor", pb_free[grp + dsl])
                    for kf in range(FC):
                        for dsl in range(4):
                            ins = t.matmul(pb[grp + dsl][:, :],
                                     b.actp[:, kf * T + ts * 128: kf * T + (ts + 1) * 128],
                                     b.wdn[:, kf * D + dsl * 512: kf * D + (dsl + 1) * 512],
                                     start=(kf == 0), stop=(kf == FC - 1))
                    tok = b.inc(ins, "pe")

                    @blk.vector
                    def _(v, ts=ts, tok=tok, grp=grp):
                        b.wait(v, "vector", tok)
                        for dsl in range(4):
                            ins = v.scalar_tensor_tensor(
                                b.r[:, ts * D + dsl * 512: ts * D + (dsl + 1) * 512],
                                pb[grp + dsl][:, :], 0.5,
                                b.r[:, ts * D + dsl * 512: ts * D + (dsl + 1) * 512],
                                op0=ALU.mult, op1=ALU.add)
                        etok = b.inc(ins, "dv")
                        for dsl in range(4):
                            pb_free[grp + dsl] = etok
            self.wdn_cons = b.cur("pe")
            self.down_war = self.wdn_cons
            if DEV_DUMP and ei == len(exps) - 1:
                self.emit_dump()
            if ei + 1 < len(exps):
                @blk.gpsimd
                def _(g, ei=ei):
                    wdown_stream(g, ei + 1)
            if call == 0 and ei < 8:
                @blk.gpsimd
                def _(g, ei=ei):
                    qtr = KD * 512
                    if ei < 4:
                        self.ckq[ei] = b.inc(g.dma_start(
                            out=self.wqb_d[:, ei * qtr:(ei + 1) * qtr],
                            in_=self.wq_d[:, ei * qtr:(ei + 1) * qtr]),
                            "wg", 16)
                    else:
                        self.ckv[ei - 4] = b.inc(g.dma_start(
                            out=self.wvb_d[:, (ei - 4) * qtr:(ei - 3) * qtr],
                            in_=self.wv_d[:, (ei - 4) * qtr:(ei - 3) * qtr]),
                            "wg", 16)
        pes2.close()

    def emit_dump(self):
        b, blk = self, self.blk

        @blk.gpsimd
        def _(g):
            b.wait(g, "gpsimd", b.cur("dv"))
            b.wait(g, "gpsimd", b.cur("ac"))
            b.wait(g, "gpsimd", b.cur("pe"))
            b.inc(g.dma_start(out=self.dbg_actp, in_=b.actp[:, :]), "io", 16)
            b.inc(g.dma_start(out=self.dbg_ht, in_=b.act_fm[:, :]), "io", 16)
            b.inc(g.dma_start(out=self.dbg_sg, in_=b.sg[:, :]), "io", 16)
            b.inc(g.dma_start(out=self.dbg_wbc, in_=b.wbc[:, :]), "io", 16)
            b.inc(g.dma_start(out=self.dbg_wt, in_=b.wT[:, :]), "io", 16)
            b.inc(g.dma_start(out=self.dbg_httm, in_=b.httm[:, :]), "io", 16)
            b.inc(g.dma_start(out=self.dbg_st8, in_=b.st8[:, :]), "io", 16)
            g.wait_ge(b.sem_h["io"], b.sem_n["io"])

    def emit_attn(self):
        b, blk, nc = self, self.blk, self.nc
        pes = ExitStack()
        pt = [pes.enter_context(nc.psum_tensor(f"apt{j}", [128, 1024], BF16))
              for j in range(2)]
        plb = pes.enter_context(nc.psum_tensor("plb", [128, 512], F32))
        self.pt_ready, self.pt_free = {}, {}

        @blk.scalar
        def _(sc):
            b.wait(sc, "scalar", b.prelude_io)
            b.inc(sc.activation(b.betin[:, :], b.betin[:, :], AF.Sigmoid), "ac")

        @blk.tensor
        def _(t):
            b.wait(t, "tensor", b.cur("ac"))
            b.wait(t, "tensor", b.cur("dv"))
            b.inc(t.transpose(plb[:, 0:16], b.betin[:, :],
                              b.eyef[0:16, 0:16]), "pe")

        @blk.vector
        def _(v):
            b.wait(v, "vector", b.cur("pe"))
            b.chain(v, "vector", "dv", v.tensor_copy(b.beta[:, :], plb[:, 0:16]))
            b.inc(v.tensor_scalar(b.beta1m[:, :], b.beta[:, :], -1.0, 1.0,
                                  op0=ALU.mult, op1=ALU.add), "dv")

        if not self.ckq:
            @blk.gpsimd
            def _(g):
                qtr = KD * 512
                for grp in range(4):
                    self.ckq[grp] = b.inc(g.dma_start(
                        out=self.wqb_d[:, grp * qtr:(grp + 1) * qtr],
                        in_=self.wq_d[:, grp * qtr:(grp + 1) * qtr]),
                        "wg", 16)
                    self.ckv[grp] = b.inc(g.dma_start(
                        out=self.wvb_d[:, grp * qtr:(grp + 1) * qtr],
                        in_=self.wv_d[:, grp * qtr:(grp + 1) * qtr]),
                        "wg", 16)

        @blk.sync
        def _(sy, ffn1_tok=b.cur("dv")):
            b.wait(sy, "sync", ffn1_tok)
            for i in range(4):
                b.inc(sy.dma_start(out=b.rsp_d[i * 128:(i + 1) * 128, :],
                                   in_=b.r[:, i * D:(i + 1) * D]), "io", 16)
        spill_tok = b.cur("io")

        @blk.scalar
        def _(sc, ffn1_tok=b.cur("dv")):
            b.wait(sc, "scalar", ffn1_tok)
            for i in range(4, NT):
                b.inc(sc.dma_start(out=b.rsp_d[i * 128:(i + 1) * 128, :],
                                   in_=b.r[:, i * D:(i + 1) * D]), "sc", 16)
        spill_tok2 = b.cur("sc")

        xln_tok = self.emit_norm(b.act_fm, rms=False, pt=pt)
        pes.close()

        pes2 = ExitStack()
        pb = [pes2.enter_context(nc.psum_tensor(f"qpb{j}", [128, 512], F32))
              for j in range(8)]
        pb_free = {j: None for j in range(8)}
        slot_sem = ["s0", "s1"]
        qkv_cons = [b.cur("pe"), b.cur("pe")]
        wlist = [(self.wkb_d, b.k_fm), (self.wvb_d, b.v_tm),
                 (self.wqb_d, b.q_fm)]

        def qkv_group(gi, pb, pb_free):
            wi, grp = gi // 4, gi % 4
            slot = gi % 2
            wd, dst = wlist[wi]
            cast_tok = (None if wi == 0 else
                        (self.ckv[grp] if wi == 1 else self.ckq[grp]))

            if wi == 0:
                @blk.gpsimd
                def _(g, slot=slot, grp=grp):
                    b.wait(g, "gpsimd", spill_tok)
                    b.wait(g, "gpsimd", spill_tok2)
                    b.wait(g, "gpsimd", qkv_cons[slot])
                    self.qkv_fill = b.inc(g.dma_start(
                        out=b.wqkv[slot][:, :],
                        in_=self.wk_d[:, grp * (KD * 512):(grp + 1) * (KD * 512)]),
                        slot_sem[slot], 16)
                fill_tok = self.qkv_fill
            elif slot == 0:
                @blk.sync
                def _(sy, slot=slot, wd=wd, grp=grp, cast_tok=cast_tok):
                    b.wait(sy, "sync", spill_tok)
                    b.wait(sy, "sync", spill_tok2)
                    b.wait(sy, "sync", cast_tok)
                    b.wait(sy, "sync", qkv_cons[slot])
                    self.qkv_fill = b.inc(sy.dma_start(
                        out=b.wqkv[slot][:, :],
                        in_=wd[:, grp * (KD * 512):(grp + 1) * (KD * 512)]),
                        slot_sem[slot], 16)
                fill_tok = self.qkv_fill
            else:
                @blk.scalar
                def _(sc, slot=slot, wd=wd, grp=grp, cast_tok=cast_tok):
                    b.wait(sc, "scalar", spill_tok)
                    b.wait(sc, "scalar", spill_tok2)
                    b.wait(sc, "scalar", cast_tok)
                    b.wait(sc, "scalar", qkv_cons[slot])
                    self.qkv_fill = b.inc(sc.dma_start(
                        out=b.wqkv[slot][:, :],
                        in_=wd[:, grp * (KD * 512):(grp + 1) * (KD * 512)]),
                        slot_sem[slot], 16)
                fill_tok = self.qkv_fill

            if wi == 1:
                @blk.tensor
                def _(t, slot=slot, grp=grp, fill_tok=fill_tok):
                    b.wait(t, "tensor", fill_tok)
                    b.wait(t, "tensor", xln_tok)
                    b.wait(t, "tensor", self.qkv_bar_dv)
                    b.wait(t, "tensor", self.qkv_bar_ac)
                    for ts in range(NT):
                        bk = ts % len(pb)
                        b.wait(t, "tensor", pb_free[bk])
                        for kk in range(KD):
                            ins = t.matmul(pb[bk][:, :],
                                     b.act_fm[:, kk * T + ts * 128: kk * T + (ts + 1) * 128],
                                     b.wqkv[slot][:, kk * 512:(kk + 1) * 512],
                                     start=(kk == 0), stop=(kk == KD - 1))
                        tok = b.inc(ins, "pe")
                        qkv_cons[slot] = tok

                        @blk.vector
                        def _(v, ts=ts, bk=bk, grp=grp, tok=tok):
                            b.wait(v, "vector", tok)
                            ins = v.tensor_copy(
                                b.v_tm[:, ts * D + grp * 512: ts * D + (grp + 1) * 512],
                                pb[bk][:, :])
                            pb_free[bk] = b.inc(ins, "dv")
                return qkv_cons[slot]

            @blk.tensor
            def _(t, slot=slot, grp=grp, dst=dst, fill_tok=fill_tok):
                b.wait(t, "tensor", fill_tok)
                b.wait(t, "tensor", xln_tok)
                b.wait(t, "tensor", self.qkv_bar_dv)
                b.wait(t, "tensor", self.qkv_bar_ac)
                for j in range(8):
                    b.wait(t, "tensor", pb_free[j])
                for kk in range(KD):
                    for dd in range(4):
                        for tt in range(2):
                            ins = t.matmul(pb[dd * 2 + tt][:, :],
                                     b.wqkv[slot][:, kk * 512 + dd * 128: kk * 512 + (dd + 1) * 128],
                                     b.act_fm[:, kk * T + tt * 512: kk * T + (tt + 1) * 512],
                                     start=(kk == 0), stop=(kk == KD - 1))
                tok = b.inc(ins, "pe")
                qkv_cons[slot] = tok

                @blk.vector
                def _(v, grp=grp, dst=dst, tok=tok):
                    b.wait(v, "vector", tok)
                    for dd in range(4):
                        for tt in range(2):
                            col = (grp * 4 + dd) * T + tt * 512
                            ins = v.tensor_copy(dst[:, col:col + 512],
                                                pb[dd * 2 + tt][:, :])
                    etok = b.inc(ins, "dv")
                    for j in range(8):
                        pb_free[j] = etok
            return qkv_cons[slot]

        q_slot_pe = {}
        self.qfill = {}
        self.qkv_bar_dv = None
        self.qkv_bar_ac = None
        self.k_tok = None
        for gi in range(4):
            qkv_group(gi, pb, pb_free)
        self.k_tok = b.cur("dv")

        for c4 in range(8):
            pairs = [(hs // H, hs % H) for hs in range(c4 * 4, c4 * 4 + 4)]

            @blk.scalar
            def _(sc, pairs=pairs, c4=c4):
                b.wait(sc, "scalar", self.k_tok)
                b.wait(sc, "scalar", spill_tok)
                b.wait(sc, "scalar", spill_tok2)
                for q2, (seg, h) in enumerate(pairs):
                    sc.activation(b.fsc[:, q2 * 512:(q2 + 1) * 512],
                                  b.k_fm[:, h * T + seg * 512:
                                         h * T + (seg + 1) * 512],
                                  AF.Relu, scale=-1.0)
                for q2 in range(4):
                    sc.activation(b.fsc[:, q2 * 512:(q2 + 1) * 512],
                                  b.fsc[:, q2 * 512:(q2 + 1) * 512],
                                  AF.Exp, scale=-1.0,
                                  accum_out=b.zsb[:, 64 + c4 * 4 + q2:
                                                  65 + c4 * 4 + q2])
                for q2, (seg, h) in enumerate(pairs):
                    ins = sc.activation(
                        b.fsc[:, q2 * 512:(q2 + 1) * 512],
                        b.k_fm[:, h * T + seg * 512: h * T + (seg + 1) * 512],
                        AF.Relu,
                        accum_out=b.zsb[:, 96 + c4 * 4 + q2:
                                        97 + c4 * 4 + q2])
                b.inc(ins, "ac")
        zinc_ac = b.cur("ac")

        @blk.vector
        def _(v):
            b.wait(v, "vector", zinc_ac)
            b.inc(v.tensor_tensor(b.zsb[:, 0:2 * H], b.zsb[:, 64:96],
                                  b.zsb[:, 96:128], op=ALU.add), "dv")
        zs_tok = b.cur("dv")

        @blk.scalar
        def _(sc):
            b.wait(sc, "scalar", zs_tok)
            b.inc(sc.dma_start(out=self.cc1i_t.ap(), in_=b.zsb[:, 0:2 * H]),
                  "io", 16)
        cc1_st = b.cur("io")

        self.qh_tok = {}
        for gi in (8, 9):
            tokpe = qkv_group(gi, pb, pb_free)
            q_slot_pe[gi % 2] = tokpe
            self.qh_tok[gi - 8] = b.cur("dv")

        @blk.gpsimd
        def _(g):
            b.wait(g, "gpsimd", cc1_st)
            b.inc(g.collective_compute(
                "AllGather", ALU.bypass, replica_groups=[list(range(NCORES))],
                ins=[self.cc1i_t.ap().opt()], outs=[self.cc1o_t.ap().opt()]),
                "cc")
            g.wait_ge(b.sem_h["cc"], b.sem_n["cc"])
            b.inc(g.dma_start(
                out=b.zsb.rearrange("p (r c) -> p r c", r=NCORES),
                in_=self.cc1o_t.ap().rearrange("(r p) c -> p r c", p=128)),
                "sc", 16)
        cc1_tok = b.cur("cc")
        zsb_tok = b.cur("sc")

        for gi in (10, 11):
            tokpe = qkv_group(gi, pb, pb_free)
            q_slot_pe[gi % 2] = tokpe
            self.qh_tok[gi - 8] = b.cur("dv")
        q_tok = b.cur("dv")

        @blk.vector
        def _(v):
            b.wait(v, "vector", zsb_tok)
            b.chain(v, "vector", "dv", v.memset(b.z_a[:, :], 1.0 / DK))
            b.chain(v, "vector", "dv", v.memset(b.z_b[:, :], 1.0 / DK))
            for j in range(NSEG):
                sl = b.zsb[:, j * H:(j + 1) * H]
                b.chain(v, "vector", "dv",
                        v.scalar_tensor_tensor(b.z_a[:, :], sl,
                                               b.zmask[:, j:j + 1],
                                               b.z_a[:, :],
                                               op0=ALU.mult, op1=ALU.add))
                b.chain(v, "vector", "dv",
                        v.scalar_tensor_tensor(b.z_b[:, :], sl,
                                               b.zmask[:, NSEG + j:NSEG + j + 1],
                                               b.z_b[:, :],
                                               op0=ALU.mult, op1=ALU.add))
            v.tensor_copy(b.z16[:, 0:H], b.z_a[:, :])
            b.inc(v.tensor_copy(b.z16[:, H:2 * H], b.z_b[:, :]), "dv")
        z_tok = b.cur("dv")

        pes2.close()
        self.qkv_bar_dv = q_tok
        pes3 = ExitStack()
        gsig = [pes3.enter_context(nc.psum_tensor(f"gsig{j}", [128, 1024], BF16))
                for j in range(2)]
        gdd = pes3.enter_context(nc.psum_tensor("gdd", [128, 512], F32))
        ggb = [pes3.enter_context(nc.psum_tensor(f"ggb{j}", [128, 512], F32))
               for j in range(2)]
        pes2v = ExitStack()
        pbv = [pes2v.enter_context(nc.psum_tensor(f"vpb{j}", [128, 512], F32))
               for j in range(3)]
        pbv_free = {j: None for j in range(3)}
        tr_free = {0: None, 1: None}
        gdd_free = [None]
        ggb_free = {0: None, 1: None}
        gstg_war = {0: None, 1: None}
        prev_dmm = {0: None, 1: None}

        def gb_sigma(seg, h0, par):
            kseg = [b.k_fm[:, (h0 + hh) * T + seg * 512:
                           (h0 + hh) * T + (seg + 1) * 512] for hh in range(2)]

            @blk.vector
            def _(v, kseg=kseg, par=par):
                b.wait(v, "vector", z_tok)
                b.wait(v, "vector", prev_dmm[par])
                for hh in range(2):
                    ins = v.tensor_scalar(b.fsc[:, hh * 512:(hh + 1) * 512],
                                          kseg[hh], 0.0, None, op0=ALU.min)
                b.inc(ins, "dv")

            @blk.scalar
            def _(sc):
                b.wait(sc, "scalar", b.cur("dv"))
                b.inc(sc.activation(b.fsc[:, 0:1024], b.fsc[:, 0:1024],
                                    AF.Exp), "ac")

            @blk.vector
            def _(v, kseg=kseg, par=par):
                b.wait(v, "vector", b.cur("ac"))
                for hh in range(2):
                    v.tensor_scalar(b.fsc[:, 1024 + hh * 512:1024 + (hh + 1) * 512],
                                    kseg[hh], 0.0, None, op0=ALU.max)
                for hh in range(2):
                    ins = v.tensor_tensor(
                        b.skf2[par][:, hh * 512:(hh + 1) * 512],
                        b.fsc[:, hh * 512:(hh + 1) * 512],
                        b.fsc[:, 1024 + hh * 512:1024 + (hh + 1) * 512],
                        op=ALU.add)
                b.inc(ins, "dv")
            return b.cur("dv")

        def gb_pe(seg, h0, par, vg, sk_tok):
            @blk.tensor
            def _(t, par=par, sk_tok=sk_tok, seg=seg, h0=h0, vg=vg):
                b.wait(t, "tensor", sk_tok)
                b.wait(t, "tensor", vg)
                b.wait(t, "tensor", tr_free[par])
                b.wait(t, "tensor", gdd_free[0])
                for hh in range(2):
                    for j in range(4):
                        t.transpose(gsig[par][:, (hh * 4 + j) * 128:(hh * 4 + j + 1) * 128],
                                    b.skf2[par][:, hh * 512 + j * 128: hh * 512 + (j + 1) * 128],
                                    b.eyeb[:, :])
                for hh in range(2):
                    zcol = b.z16[:, seg * H + h0 + hh: seg * H + h0 + hh + 1]
                    for j in range(4):
                        ins = t.matmul(gdd[:, hh * 4 + j: hh * 4 + j + 1],
                                       b.skf2[par][:, hh * 512 + j * 128: hh * 512 + (j + 1) * 128],
                                       zcol, start=True, stop=True)
                b.inc(ins, "pe")
            trd_pe = b.cur("pe")
            prev_dmm[par] = trd_pe

            @blk.vector
            def _(v, par=par, trd_pe=trd_pe):
                b.wait(v, "vector", trd_pe)
                b.wait(v, "vector", gstg_war[par])
                tok = b.inc(v.tensor_copy(b.skt2[par][:, :], gsig[par][:, :]),
                            "dv")
                tr_free[par] = tok
                ins = v.reciprocal(b.rdc[:, 0:8], gdd[:, 0:8])
                tok2 = b.inc(ins, "dv")
                b.wait(v, "vector", tok2)
                for q2 in range(8):
                    ins = v.tensor_scalar(
                        b.sks2[par][:, q2 * 128:(q2 + 1) * 128],
                        b.skt2[par][:, q2 * 128:(q2 + 1) * 128],
                        b.rdc[:, q2:q2 + 1], None, op0=ALU.mult)
                tok3 = b.inc(ins, "dv")
            gdd_free[0] = b.cur("dv")
            sc_tok = gdd_free[0]

            @blk.tensor
            def _(t, par=par, sc_tok=sc_tok, seg=seg, h0=h0):
                b.wait(t, "tensor", sc_tok)
                b.wait(t, "tensor", ggb_free[par])
                for hh in range(2):
                    for j in range(4):
                        t.matmul(ggb[par][:, hh * 256: hh * 256 + 128],
                                 b.skt2[par][:, (hh * 4 + j) * 128:(hh * 4 + j + 1) * 128],
                                 b.sks2[par][:, (hh * 4 + j) * 128:(hh * 4 + j + 1) * 128],
                                 start=(j == 0), stop=(j == 3))
                    for j in range(4):
                        vcol = (seg * 4 + j) * D + (h0 + hh) * 128
                        ins = t.matmul(ggb[par][:, hh * 256 + 128: hh * 256 + 256],
                                 b.skt2[par][:, (hh * 4 + j) * 128:(hh * 4 + j + 1) * 128],
                                 b.v_tm[:, vcol:vcol + 128],
                                 start=(j == 0), stop=(j == 3))
                b.inc(ins, "pe")
            gb_pe_tok = b.cur("pe")

            @blk.vector
            def _(v, par=par, gb_pe_tok=gb_pe_tok):
                b.wait(v, "vector", gb_pe_tok)
                for hh in range(2):
                    v.tensor_tensor(b.gstg[par][:, hh * 256: hh * 256 + 128],
                                    b.eyef[:, :],
                                    ggb[par][:, hh * 256: hh * 256 + 128],
                                    op=ALU.subtract)
                    ins = v.tensor_copy(
                        b.gstg[par][:, hh * 256 + 128: hh * 256 + 256],
                        ggb[par][:, hh * 256 + 128: hh * 256 + 256])
                tok = b.inc(ins, "dv")
                ggb_free[par] = tok

            @blk.sync
            def _(sy, par=par, seg=seg, h0=h0):
                b.wait(sy, "sync", b.cur("dv"))
                gstg_war[par] = b.inc(sy.dma_start(
                    out=self.cc2i_t.ap()[:, seg * (H * 256) + h0 * 256:
                                         seg * (H * 256) + (h0 + 2) * 256],
                    in_=b.gstg[par][:, :]), f"g{par}", 16)

        GBs = []
        for grp in range(4):
            for h0, seg in ((4 * grp, 0), (4 * grp, 1),
                            (4 * grp + 2, 0), (4 * grp + 2, 1)):
                GBs.append((seg, h0, grp))
        sk_toks = {}
        vgt = {}
        for grp in range(4):
            qkv_group(4 + grp, pbv, pbv_free)
            vgt[grp] = b.cur("dv")
            for q in range(4):
                G = 4 * grp + q
                if G >= 2:
                    gp = G - 2
                    gb_pe(GBs[gp][0], GBs[gp][1], gp % 2,
                          vgt[GBs[gp][2]], sk_toks[gp])
                sk_toks[G] = gb_sigma(GBs[G][0], GBs[G][1], G % 2)
        for gp in (14, 15):
            gb_pe(GBs[gp][0], GBs[gp][1], gp % 2,
                  vgt[GBs[gp][2]], sk_toks[gp])
        v_tok = b.cur("dv")
        pes2v.close()
        gb_all = b.cur("g0")
        gb_all2 = b.cur("g1")
        pes3.close()

        @blk.gpsimd
        def _(g):
            b.wait(g, "gpsimd", gb_all)
            b.wait(g, "gpsimd", gb_all2)
            b.inc(g.collective_compute(
                "AllGather", ALU.bypass, replica_groups=[list(range(NCORES))],
                ins=[self.cc2i_t.ap().opt()], outs=[self.cc2o_t.ap().opt()]),
                "cc")
        cc2_tok = b.cur("cc")

        @blk.scalar
        def _(sc):
            b.wait(sc, "scalar", qkv_cons[0])
            b.wait(sc, "scalar", qkv_cons[1])
            b.inc(sc.dma_start(out=b.maskt[:, :], in_=b.maskt_d[:, :]), "sc", 16)
        maskt_tok = b.cur("sc")

        pes4a = ExitStack()
        pAT = pes4a.enter_context(nc.psum_tensor("pAT", [128, 512], F32))
        pDT = pes4a.enter_context(nc.psum_tensor("pDT", [128, 16], F32))
        cpt = pes4a.enter_context(nc.psum_tensor("cpt", [128, 512], BF16))
        pes4b = ExitStack()
        pS = [pes4b.enter_context(nc.psum_tensor(f"pS{j}", [128, 512], F32))
              for j in range(2)]
        pRp = [pes4b.enter_context(nc.psum_tensor(f"pRp{j}", [128, 512], F32))
               for j in range(2)]

        @blk.vector
        def _(v):
            b.wait(v, "vector", z_tok)
            b.wait(v, "vector", b.cur("pe"))
            b.inc(v.memset(b.mem16[:, :], 0.0), "dv")
        memz_tok = b.cur("dv")

        ET = [b.expT, b.expT2]
        slab_cons = [q_slot_pe[0], q_slot_pe[1]]
        slab_sems = ["s2", "s3"]
        commitA = {-1: memz_tok}
        commitB = {-1: memz_tok}
        halfB_pe = {}
        exp_done = {0: None, 1: None}
        expT_war = {}
        sc_pe = {}
        sq_done = {}
        sqe_ac = {-1: None}
        exp_tok = {}
        ad_pe = {-1: None}
        ev_dv = {-1: None}
        tr_pe = {}
        bl_dv = {-1: None}
        fills = {}

        def emit_fill(s):
            slot = s % 2
            if s < NROUND:
                r_, q_ = s // 2, s % 2
                src = self.cc2o_t.ap()[r_ * 128:(r_ + 1) * 128,
                                       q_ * (H * 256):(q_ + 1) * (H * 256)]
            else:
                src = self.cc2i_t.ap()[:, 0:H * 256]

            @blk.sync
            def _(sy, slot=slot, src=src):
                b.wait(sy, "sync", cc2_tok)
                b.wait(sy, "sync", slab_cons[slot])
                b.inc(sy.dma_start(out=b.slab[slot][:, :], in_=src),
                      slab_sems[slot], 16)
            fills[s] = b.cur(slab_sems[slot])

        def replay_half(s, hb):
            slot = s % 2
            own = (s == NROUND)

            @blk.tensor
            def _(t, s=s, hb=hb, slot=slot):
                b.wait(t, "tensor", fills[s])
                b.wait(t, "tensor", commitB[s - 1] if hb == 0 else commitA[s])
                sl = b.slab[slot]
                for hh in range(8):
                    h = hb * 8 + hh
                    bank, col = pRp[hh // 4], (hh % 4) * 128
                    t.matmul(bank[:, col:col + 128],
                             sl[:, h * 256: h * 256 + 128],
                             b.mem16[:, h * 128:(h + 1) * 128],
                             start=True, stop=False)
                    ins = t.matmul(bank[:, col:col + 128], b.eyeb[:, :],
                                   sl[:, h * 256 + 128: h * 256 + 256],
                                   start=False, stop=True)
                b.inc(ins, "pe")
            hpe = b.cur("pe")
            if hb == 1:
                halfB_pe[s] = hpe

            @blk.vector
            def _(v, s=s, hb=hb, hpe=hpe, own=own):
                b.wait(v, "vector", hpe)
                off = hb * 1024
                if own:
                    for b2 in range(2):
                        ins = v.tensor_copy(
                            b.mem16b[:, off + b2 * 512: off + (b2 + 1) * 512],
                            pRp[b2][:, :])
                else:
                    v.tensor_scalar(b.keep2[:, off:off + 1024],
                                    b.mem16[:, off:off + 1024],
                                    b.rmask[:, NROUND + s:NROUND + s + 1],
                                    None, op0=ALU.mult)
                    for b2 in range(2):
                        ins = v.scalar_tensor_tensor(
                            b.mem16[:, off + b2 * 512: off + (b2 + 1) * 512],
                            pRp[b2][:, :], b.rmask[:, s:s + 1],
                            b.keep2[:, off + b2 * 512: off + (b2 + 1) * 512],
                            op0=ALU.mult, op1=ALU.add)
                tok = b.inc(ins, "dv")
                if hb == 0:
                    commitA[s] = tok
                else:
                    commitB[s] = tok
                    slab_cons[s % 2] = tok

        def att_dst(h, seg):
            return b.att_all.rearrange(
                "p (cc sg col) -> p cc sg col", cc=KD, sg=2)[:, :, seg,
                                                            h * 32:(h + 1) * 32]

        def pa_scores(i, jj):
            h, seg = i // 2, i % 2
            qsl = b.q_fm[:, h * T + seg * 512: h * T + (seg + 1) * 512]
            ksl = b.k_fm[:, h * T + seg * 512: h * T + (seg + 1) * 512]
            et = ET[i % 2]
            for j in (2 * jj, 2 * jj + 1):
                bk = j % 2

                @blk.tensor
                def _(t, i=i, j=j, bk=bk, qsl=qsl, ksl=ksl):
                    b.wait(t, "tensor", self.k_tok)
                    b.wait(t, "tensor", self.qh_tok[(i // 2) // 4])
                    b.wait(t, "tensor", maskt_tok)
                    b.wait(t, "tensor", v_tok)
                    b.wait(t, "tensor", exp_done[bk])
                    t.matmul(pS[bk][:, :], ksl[:, j * 128:(j + 1) * 128], qsl,
                             start=True, stop=False)
                    ins = t.matmul(pS[bk][:, :], b.eyeb[:, :],
                                   b.maskt[:, j * 512:(j + 1) * 512],
                                   start=False, stop=True)
                    b.inc(ins, "pe")
                spe = b.cur("pe")

                @blk.scalar
                def _(sc, i=i, j=j, bk=bk, et=et, spe=spe):
                    b.wait(sc, "scalar", spe)
                    b.wait(sc, "scalar", b.prelude_io)
                    if i < 2:
                        b.wait(sc, "scalar", gb_all)
                        b.wait(sc, "scalar", gb_all2)
                    b.wait(sc, "scalar", expT_war.get(i - 2))
                    b.inc(sc.activation(et[:, j * 512:(j + 1) * 512],
                                        pS[bk][:, :], AF.Exp,
                                        bias=b.shiftc[:, 0:1], scale=SCALE),
                          "ac")
                exp_done[bk] = b.cur("ac")
            if jj == 1:
                sc_pe[i] = b.cur("pe")
                exp_tok[i] = b.cur("ac")

        def pa_sq(i):
            h, seg = i // 2, i % 2
            qsl = b.q_fm[:, h * T + seg * 512: h * T + (seg + 1) * 512]

            @blk.vector
            def _(v, i=i, qsl=qsl):
                b.wait(v, "vector", sqe_ac[i - 1])
                b.inc(v.tensor_scalar(b.fsc[:, 0:512], qsl, 0.0, None,
                                      op0=ALU.min), "dv")
            mn_dv = b.cur("dv")

            @blk.scalar
            def _(sc, mn_dv=mn_dv):
                b.wait(sc, "scalar", mn_dv)
                b.inc(sc.activation(b.fsc[:, 0:512], b.fsc[:, 0:512], AF.Exp),
                      "ac")
            sqe_ac[i] = b.cur("ac")

            @blk.vector
            def _(v, i=i, qsl=qsl):
                b.wait(v, "vector", sqe_ac[i])
                b.wait(v, "vector", sc_pe[i])
                v.tensor_scalar(b.fsc[:, 512:1024], qsl, 0.0, None,
                                op0=ALU.max)
                b.inc(v.tensor_tensor(qsl, b.fsc[:, 0:512],
                                      b.fsc[:, 512:1024], op=ALU.add), "dv")
            sq_done[i] = b.cur("dv")

        def pa_ad(i):
            h, seg = i // 2, i % 2
            et = ET[i % 2]

            @blk.tensor
            def _(t, i=i, h=h, seg=seg, et=et):
                b.wait(t, "tensor", exp_tok[i])
                b.wait(t, "tensor", v_tok)
                b.wait(t, "tensor", ev_dv[i - 1])
                for qb in range(4):
                    for j in range(4):
                        t.matmul(pAT[:, qb * 128:(qb + 1) * 128],
                                 et[:, j * 512 + qb * 128: j * 512 + (qb + 1) * 128],
                                 b.v_tm[:, (seg * 4 + j) * D + h * 128:
                                        (seg * 4 + j) * D + (h + 1) * 128],
                                 start=(j == 0), stop=(j == 3))
                    for j in range(4):
                        ins = t.matmul(pDT[:, qb:qb + 1],
                                 et[:, j * 512 + qb * 128: j * 512 + (qb + 1) * 128],
                                 b.o128[:, 0:1],
                                 start=(j == 0), stop=(j == 3))
                b.inc(ins, "pe")
            ad_pe[i] = b.cur("pe")
            expT_war[i] = ad_pe[i]

            @blk.vector
            def _(v, i=i):
                b.wait(v, "vector", ad_pe[i])
                b.wait(v, "vector", tr_pe.get(i - 2))
                tok = b.inc(v.reciprocal(b.rdc[:, 0:4], pDT[:, 0:4]), "dv")
                b.wait(v, "vector", tok)
                for qb in range(4):
                    ins = v.tensor_scalar(
                        b.attTs[i % 2][:, qb * 128:(qb + 1) * 128],
                        pAT[:, qb * 128:(qb + 1) * 128],
                        b.rdc[:, qb:qb + 1], None, op0=ALU.mult)
                b.inc(ins, "dv")
            ev_dv[i] = b.cur("dv")

        def pa_tr(i):
            h, seg = i // 2, i % 2

            @blk.tensor
            def _(t, i=i):
                b.wait(t, "tensor", ev_dv[i])
                b.wait(t, "tensor", bl_dv[i - 1])
                for qb in range(4):
                    ins = t.transpose(cpt[:, qb * 128:(qb + 1) * 128],
                                      b.attTs[i % 2][:, qb * 128:(qb + 1) * 128],
                                      b.eyeb[:, :])
                b.inc(ins, "pe")
            tr_pe[i] = b.cur("pe")

            @blk.vector
            def _(v, i=i, h=h, seg=seg):
                b.wait(v, "vector", tr_pe[i])
                v.tensor_copy(b.attC[:, :], cpt[:, :])
                src = b.attC.rearrange("p (j cc) -> p cc j", j=32)
                ins = v.tensor_scalar(att_dst(h, seg), src,
                                      b.beta1m[:, h:h + 1], None, op0=ALU.mult)
                b.inc(ins, "dv")
            bl_dv[i] = b.cur("dv")

        def phaseB(i):
            k = 2 * H + i
            h, seg = i // 2, i % 2
            sqsl = b.q_fm[:, h * T + seg * 512: h * T + (seg + 1) * 512]
            mem = b.mem16 if seg == 0 else b.mem16b

            @blk.tensor
            def _(t, i=i, k=k, h=h, seg=seg, sqsl=sqsl, mem=mem):
                b.wait(t, "tensor", commitB[NROUND])
                b.wait(t, "tensor", sq_done[i])
                b.wait(t, "tensor", ev_dv[k - 1])
                for qb in range(4):
                    t.matmul(pDT[:, qb:qb + 1],
                             sqsl[:, qb * 128:(qb + 1) * 128],
                             b.z16[:, seg * H + h: seg * H + h + 1],
                             start=True, stop=True)
                    ins = t.matmul(pAT[:, qb * 128:(qb + 1) * 128],
                             sqsl[:, qb * 128:(qb + 1) * 128],
                             mem[:, h * 128:(h + 1) * 128],
                             start=True, stop=True)
                b.inc(ins, "pe")
            ad_pe[k] = b.cur("pe")

            @blk.vector
            def _(v, k=k):
                b.wait(v, "vector", ad_pe[k])
                b.wait(v, "vector", tr_pe.get(k - 2))
                tok = b.inc(v.reciprocal(b.rdc[:, 0:4], pDT[:, 0:4]), "dv")
                b.wait(v, "vector", tok)
                for qb in range(4):
                    ins = v.tensor_scalar(
                        b.attTs[k % 2][:, qb * 128:(qb + 1) * 128],
                        pAT[:, qb * 128:(qb + 1) * 128],
                        b.rdc[:, qb:qb + 1], None, op0=ALU.mult)
                b.inc(ins, "dv")
            ev_dv[k] = b.cur("dv")

            @blk.tensor
            def _(t, k=k):
                b.wait(t, "tensor", ev_dv[k])
                b.wait(t, "tensor", bl_dv[k - 1])
                for qb in range(4):
                    ins = t.transpose(cpt[:, qb * 128:(qb + 1) * 128],
                                      b.attTs[k % 2][:, qb * 128:(qb + 1) * 128],
                                      b.eyeb[:, :])
                b.inc(ins, "pe")
            tr_pe[k] = b.cur("pe")

            @blk.vector
            def _(v, k=k, h=h, seg=seg):
                b.wait(v, "vector", tr_pe[k])
                v.tensor_copy(b.attC[:, :], cpt[:, :])
                src = b.attC.rearrange("p (j cc) -> p cc j", j=32)
                dst = att_dst(h, seg)
                ins = v.scalar_tensor_tensor(dst, src, b.beta[:, h:h + 1],
                                             dst, op0=ALU.mult, op1=ALU.add)
                b.inc(ins, "dv")
            bl_dv[k] = b.cur("dv")

        emit_fill(0)
        emit_fill(1)
        pa_scores(0, 0)
        pa_scores(0, 1)
        for i in range(2 * H):
            if i + 1 < 2 * H:
                pa_scores(i + 1, 0)
            pa_ad(i)
            if i + 1 < 2 * H:
                pa_scores(i + 1, 1)
            pa_tr(i)
            pa_sq(i)
            hx = i - 4
            if 0 <= hx < 2 * (NROUND + 1):
                r, hb = hx // 2, hx % 2
                replay_half(r, hb)
                if hb == 1 and r + 2 <= NROUND:
                    emit_fill(r + 2)
        for hx in range(2 * H - 4, 2 * (NROUND + 1)):
            replay_half(hx // 2, hx % 2)

        pes4b.close()

        combA_f = ad_pe[2 * H - 1]
        own_pe = halfB_pe[NROUND]
        woq_tok = {}

        @blk.gpsimd
        def _(g):
            b.wait(g, "gpsimd", sc_pe[15])
            woq_tok[2] = b.inc(g.dma_start(
                out=b.wo_res[:, 2 * 8192:3 * 8192],
                in_=self.wo_d[:, 2 * 8192:3 * 8192]), "w2", 16)
            b.wait(g, "gpsimd", sc_pe[2 * H - 1])
            woq_tok[3] = b.inc(g.dma_start(
                out=b.wo_res[:, 3 * 8192:4 * 8192],
                in_=self.wo_d[:, 3 * 8192:4 * 8192]), "w3", 16)

        rrel_ch = {}

        @blk.sync
        def _(sy):
            b.wait(sy, "sync", combA_f)
            b.wait(sy, "sync", own_pe)
            for i in (0, 1, 2, 3, 6):
                rrel_ch[i] = b.inc(
                    sy.dma_start(out=b.r[:, i * D:(i + 1) * D],
                                 in_=b.rsp_d[i * 128:(i + 1) * 128, :]),
                    "s0", 16)

        pes5 = ExitStack()
        pb2 = [pes5.enter_context(nc.psum_tensor(f"wpb{j}", [128, 512], F32))
               for j in range(4)]
        pb2_free = {j: None for j in range(4)}

        def wo_pass(ts, kks, blt):
            @blk.tensor
            def _(t, ts=ts, kks=kks, blt=blt):
                b.wait(t, "tensor", blt)
                for dsl in range(4):
                    b.wait(t, "tensor", pb2_free[dsl])
                for kx, kk in enumerate(kks):
                    b.wait(t, "tensor", woq_tok.get(kk // 4))
                    for dsl in range(4):
                        ins = t.matmul(pb2[dsl][:, :],
                                 b.att_all[:, kk * T + ts * 128: kk * T + (ts + 1) * 128],
                                 b.wo_res[:, kk * D + dsl * 512: kk * D + (dsl + 1) * 512],
                                 start=(kx == 0), stop=(kx == len(kks) - 1))
                tok = b.inc(ins, "pe")

                @blk.vector
                def _(v, ts=ts, tok=tok):
                    b.wait(v, "vector", tok)
                    b.wait(v, "vector", rrel_ch[ts])
                    for dsl in range(4):
                        ins = v.tensor_tensor(
                            b.r[:, ts * D + dsl * 512: ts * D + (dsl + 1) * 512],
                            b.r[:, ts * D + dsl * 512: ts * D + (dsl + 1) * 512],
                            pb2[dsl][:, :], op=ALU.add)
                    etok = b.inc(ins, "dv")
                    for dsl in range(4):
                        pb2_free[dsl] = etok

        kk_hi = list(range(8, 16))
        kk_lo = list(range(8))
        for hq in range(4):
            for h in range(4 * hq, 4 * hq + 4):
                phaseB(2 * h)
                phaseB(2 * h + 1)
            wo_pass(hq, kk_hi, bl_dv[2 * H + 2 * (4 * hq + 3) + 1])

        combB_h, combB_f = ad_pe[2 * H + 15], ad_pe[4 * H - 1]
        blB_f = bl_dv[4 * H - 1]
        comb_tok = b.cur("dv")

        @blk.gpsimd
        def _(g):
            b.wait(g, "gpsimd", combB_h)
            woq_tok[0] = b.inc(g.dma_start(
                out=b.wo_res[:, 0 * 8192:1 * 8192],
                in_=self.wo_d[:, 0 * 8192:1 * 8192]), "w0", 16)
            b.wait(g, "gpsimd", combB_f)
            woq_tok[1] = b.inc(g.dma_start(
                out=b.wo_res[:, 1 * 8192:2 * 8192],
                in_=self.wo_d[:, 1 * 8192:2 * 8192]), "w1", 16)

        @blk.scalar
        def _(sc):
            b.wait(sc, "scalar", combB_f)
            b.wait(sc, "scalar", comb_tok)
            for i in (4, 5):
                rrel_ch[i] = b.inc(
                    sc.dma_start(out=b.r[:, i * D:(i + 1) * D],
                                 in_=b.rsp_d[i * 128:(i + 1) * 128, :]),
                    "s1", 16)

        @blk.sync
        def _(sy):
            b.wait(sy, "sync", tr_pe[4 * H - 1])
            b.wait(sy, "sync", comb_tok)
            rrel_ch[7] = b.inc(
                sy.dma_start(out=b.r[:, 7 * D:8 * D],
                             in_=b.rsp_d[7 * 128:8 * 128, :]), "s0", 16)

        for ts in (4, 5, 6, 7):
            wo_pass(ts, kk_hi, blB_f)
        for ts in range(NT):
            wo_pass(ts, kk_lo, blB_f)
        pes5.close()
        pes4a.close()

    def emit_store(self):
        b, blk = self, self.blk
        fin = b.cur("dv")

        @blk.sync
        def _(sy):
            b.wait(sy, "sync", fin)
            for i in range(NT):
                b.inc(sy.dma_start(out=b.out_d[i * 128:(i + 1) * 128, :],
                                   in_=b.r[:, i * D:(i + 1) * D]), "st", 16)
            sy.wait_ge(b.sem_h["st"], b.sem_n["st"])


def _consts():
    sel8 = np.zeros((8, 8 * 128), dtype=ml_dtypes.bfloat16)
    for e in range(8):
        sel8[e, e * 128:(e + 1) * 128] = 1.0
    eyeb = np.eye(128, dtype=ml_dtypes.bfloat16)
    eyef = np.eye(128, dtype=np.float32)
    o128 = np.ones((128, 1), dtype=ml_dtypes.bfloat16)
    o1 = np.ones((1, 128), dtype=ml_dtypes.bfloat16)
    mt = np.zeros((128, 4 * SEG), dtype=np.float32)
    for j in range(4):
        kt = j * 128 + np.arange(128)[:, None]
        qt = np.arange(SEG)[None, :]
        mt[:, j * SEG:(j + 1) * SEG] = np.where(kt <= qt, 0.0, -1e9 / SCALE)
    return eyeb, eyef, o128, o1, mt.astype(ml_dtypes.bfloat16), sel8


def _core_masks(c):
    zm = np.zeros((128, 2 * NSEG), dtype=np.float32)
    for j in range(NSEG):
        zm[:, j] = 1.0 if j < 2 * c else 0.0
        zm[:, NSEG + j] = 1.0 if j < 2 * c + 1 else 0.0
    rm = np.zeros((128, 2 * NROUND), dtype=np.float32)
    for s in range(NROUND):
        m = 1.0 if s < 2 * c else 0.0
        rm[:, s] = m
        rm[:, NROUND + s] = 1.0 - m
    return zm, rm


def kernel(L0_attn_x, L0_attn_Wq, L0_attn_Wk, L0_attn_Wv, L0_attn_Wo,
           L0_attn_betas, L0_ffn_rms_weight, L0_ffn_router_w, L0_ffn_w_gate,
           L0_ffn_w_up, L0_ffn_w_down, L0_ffn_sw_gate, L0_ffn_sw_up,
           L0_ffn_sw_down, _parts=("ffn1", "attn", "ffn2")):
    x = np.ascontiguousarray(np.asarray(L0_attn_x, dtype=np.float32).reshape(S, D))
    eyeb, eyef, o128, o1, mt, sel8 = _consts()
    def _pm_qkv(w):
        return np.ascontiguousarray(
            np.asarray(w, np.float32).reshape(KD, 128, 4, 512)
            .transpose(1, 2, 0, 3).reshape(128, 4 * KD * 512))

    def _pm_half(w):
        return np.ascontiguousarray(
            np.asarray(w, np.float32).reshape(KD, 128, 2, 512)
            .transpose(1, 2, 0, 3).reshape(128, 2 * KD * 512))

    def _pm_down(w):
        return np.ascontiguousarray(
            np.asarray(w, np.float32).reshape(FC, 128, D)
            .transpose(1, 0, 2).reshape(128, FC * D))

    def _pm_wo(w):
        return np.ascontiguousarray(
            np.asarray(w, np.float32).reshape(KD, 128, D)
            .transpose(1, 0, 2).reshape(128, KD * D))

    common = {
        "wq": _pm_qkv(L0_attn_Wq),
        "wk": _pm_qkv(L0_attn_Wk),
        "wv": _pm_qkv(L0_attn_Wv),
        "wo": _pm_wo(L0_attn_Wo),
        "betas": np.ascontiguousarray(np.asarray(L0_attn_betas, np.float32).reshape(H, DV)),
        "rmsw": np.ascontiguousarray(np.asarray(L0_ffn_rms_weight, np.float32).reshape(KD, 128).T),
        "router": np.ascontiguousarray(
            np.asarray(L0_ffn_router_w, np.float32).reshape(KD, 128, E)
            .transpose(1, 0, 2).reshape(128, KD * E)),
        "wg": np.stack([_pm_half(L0_ffn_w_gate[e]) for e in range(E)]),
        "wu": np.stack([_pm_half(L0_ffn_w_up[e]) for e in range(E)]),
        "wd": np.stack([_pm_down(L0_ffn_w_down[e]) for e in range(E)]),
        "swg": _pm_half(L0_ffn_sw_gate),
        "swu": _pm_half(L0_ffn_sw_up),
        "swd": _pm_down(L0_ffn_sw_down),
        "eyeb": eyeb, "eyef": eyef, "o128": o128, "o1": o1, "maskt": mt,
        "sel8": sel8,
        "shiftc": np.full((128, 1), SHIFT, dtype=np.float32),
    }
    in_maps = []
    for c in range(NCORES):
        zm, rm = _core_masks(c)
        m = dict(common)
        m["x"] = np.ascontiguousarray(x[c * T:(c + 1) * T])
        m["zmaskc"] = zm
        m["rmaskc"] = rm
        in_maps.append(m)

    bld = B()
    nc = bld.build(parts=_parts)
    res = run_bass_kernel_spmd(nc, in_maps, core_ids=list(range(NCORES)))
    out = np.concatenate([np.asarray(r["out"], dtype=np.float32)
                          for r in res.results], axis=0)
    return out.reshape(1, S, D)


# version 28
# speedup vs baseline: 1.0133x; 1.0133x over previous
import numpy as np
import ml_dtypes
from contextlib import ExitStack

import concourse.bass as bass
import concourse.mybir as mybir
from concourse.bass_utils import run_bass_kernel_spmd

F32 = mybir.dt.float32
BF16 = mybir.dt.bfloat16
AF = mybir.ActivationFunctionType
ALU = mybir.AluOpType
AX = mybir.AxisListType

NCORES = 8
S, D = 8192, 2048
T = S // NCORES
NT = T // 128
KD = D // 128
H, DK, DV = 16, 128, 128
SEG = 512
NSEG = S // SEG
E, FF = 8, 1024
FC = FF // 128
SCALE = 1.0 / float(DK) ** 0.5
SHIFT = -16.0
LNEPS = 1e-5
RMSEPS = 1e-6
NROUND = NSEG - 2
DEV_FFN_STAGE = "full"
DEV_NEXP = None
DEV_DUMP = False

P0 = 17408
AFB = 29952
A1 = 62720
A2 = 95488
A3 = 128256
RR = 161024


class B:
    def __init__(self):
        self.nc = bass.Bass("TRN2", target_bir_lowering=False,
                            detect_race_conditions=False)
        self.es = ExitStack()
        self.sem_h, self.sem_n, self.last_wait = {}, {}, {}

    def sems(self, names):
        for n in names:
            self.sem_h[n] = self.es.enter_context(self.nc.semaphore(n))
            self.sem_n[n] = 0

    def inc(self, instr, name, k=1):
        instr.then_inc(self.sem_h[name], k)
        self.sem_n[name] += k
        return (name, self.sem_n[name])

    def cur(self, name):
        return (name, self.sem_n[name])

    def wait(self, eng, ename, tok):
        if tok is None:
            return
        sname, val = tok
        if val <= 0:
            return
        key = (ename, sname)
        if self.last_wait.get(key, -1) >= val:
            return
        eng.wait_ge(self.sem_h[sname], val)
        self.last_wait[key] = val

    def chain(self, eng, ename, dom, ins):
        self.wait(eng, ename, self.inc(ins, dom))

    def sb(self, name, shape, dtype, off):
        return self.nc.alloc_sbuf_tensor_at(name, shape, dtype, offset=off)

    def build(self, parts=("ffn1", "attn", "ffn2")):
        nc = self.nc
        d = lambda n, s, dt, k: nc.dram_tensor(n, s, dt, kind=k).ap()
        self.x_d = d("x", [T, D], F32, "ExternalInput")
        self.wq_d = d("wq", [128, 4 * KD * 512], F32, "ExternalInput")
        self.wk_d = d("wk", [128, 4 * KD * 512], F32, "ExternalInput")
        self.wv_d = d("wv", [128, 4 * KD * 512], F32, "ExternalInput")
        self.wo_d = d("wo", [128, KD * D], F32, "ExternalInput")
        self.betas_d = d("betas", [H, DV], F32, "ExternalInput")
        self.rmsw_d = d("rmsw", [128, KD], F32, "ExternalInput")
        self.router_d = d("router", [128, KD * E], F32, "ExternalInput")
        self.wg_d = d("wg", [E, 128, 2 * KD * 512], F32, "ExternalInput")
        self.wu_d = d("wu", [E, 128, 2 * KD * 512], F32, "ExternalInput")
        self.wd_d = d("wd", [E, 128, FC * D], F32, "ExternalInput")
        self.swg_d = d("swg", [128, 2 * KD * 512], F32, "ExternalInput")
        self.swu_d = d("swu", [128, 2 * KD * 512], F32, "ExternalInput")
        self.swd_d = d("swd", [128, FC * D], F32, "ExternalInput")
        self.eyeb_d = d("eyeb", [128, 128], BF16, "ExternalInput")
        self.eyef_d = d("eyef", [128, 128], F32, "ExternalInput")
        self.o128_d = d("o128", [128, 1], BF16, "ExternalInput")
        self.o1_d = d("o1", [1, 128], BF16, "ExternalInput")
        self.sel8_d = d("sel8", [8, 8 * 128], BF16, "ExternalInput")
        self.shiftc_d = d("shiftc", [128, 1], F32, "ExternalInput")
        self.maskt_d = d("maskt", [128, 4 * SEG], BF16, "ExternalInput")
        self.zmask_d = d("zmaskc", [128, 2 * NSEG], F32, "ExternalInput")
        self.rmask_d = d("rmaskc", [128, 2 * NROUND], F32, "ExternalInput")
        self.out_d = d("out", [T, D], F32, "ExternalOutput")
        if DEV_DUMP:
            self.dbg_ht = d("dbg_ht", [128, KD * T], F32, "ExternalOutput")
            self.dbg_actp = d("dbg_actp", [128, FC * T], F32, "ExternalOutput")
            self.dbg_sg = d("dbg_sg", [128, FC * SEG], F32, "ExternalOutput")
            self.dbg_wbc = d("dbg_wbc", [128, T], F32, "ExternalOutput")
            self.dbg_wt = d("dbg_wt", [8, T], F32, "ExternalOutput")
            self.dbg_httm = d("dbg_httm", [128, D], F32, "ExternalOutput")
            self.dbg_lnx = d("dbg_lnx", [128, D], F32, "ExternalOutput")
            self.dbg_st8 = d("dbg_st8", [128, 64], F32, "ExternalOutput")
        self.rsp_d = self.nc.dram_tensor("rspill", [T, D], F32).ap()
        self.wqb_d = self.nc.dram_tensor("wqb", [128, 4 * KD * 512], BF16).ap()
        self.wkb_d = self.nc.dram_tensor("wkb", [128, 4 * KD * 512], BF16).ap()
        self.wvb_d = self.nc.dram_tensor("wvb", [128, 4 * KD * 512], BF16).ap()
        self.cc1i_t = nc.dram_tensor("cc1i", [128, 2 * H], F32)
        self.cc1o_t = nc.dram_tensor("cc1o", [NCORES * 128, 2 * H], F32)
        self.cc2i_t = nc.dram_tensor("cc2i", [128, H * SEG], BF16)
        self.cc2o_t = nc.dram_tensor("cc2o", [NCORES * 128, H * SEG], BF16)

        o = P0
        def al(name, shape, dtype, nbytes):
            nonlocal o
            t = self.sb(name, shape, dtype, o)
            o += (nbytes + 63) & ~63
            return t
        self.eyeb = al("eyeb_s", [128, 128], BF16, 256)
        self.eyef = al("eyef_s", [128, 128], F32, 512)
        self.o128 = al("o128_s", [128, 1], BF16, 32)
        self.o1 = al("o1_s", [1, 128], BF16, 256)
        self.sel8 = al("sel8_s", [8, 8 * 128], BF16, 2048)
        self.routf = al("routf", [128, KD * E], BF16, 256)
        self.rmsc = al("rmsc", [128, KD], F32, 64)
        self.beta = al("beta_s", [128, H], F32, 64)
        betin_off = o
        self.betin = al("betin", [16, 128], F32, 512)
        self.beta1m = self.sb("beta1m", [128, H], F32, betin_off)
        self.logit = al("logit", [8, T], F32, 4096)
        self.rsc = al("rsc", [128, 40], F32, 160)
        self.wde = al("wde", [128, NT * E], F32, 256)
        self.wde16 = al("wde16", [128, NT * E], BF16, 128)
        self.wT = al("wT", [8, T], BF16, 2048)
        self.zsb = al("zsb", [128, NCORES * 2 * H], F32, 1024)
        self.z_a = al("z_a", [128, H], F32, 64)
        self.z_b = al("z_b", [128, H], F32, 64)
        self.z16 = al("z16", [128, 2 * H], BF16, 64)
        self.zmask = al("zmask", [128, 2 * NSEG], F32, 128)
        self.rmask = al("rmask", [128, 2 * NROUND], F32, 128)
        self.st8 = al("st8", [128, 64], F32, 256)
        self.shiftc = al("shiftc", [128, 1], F32, 32)
        assert o <= AFB, o

        sb = self.sb
        self.act_fm = sb("act_fm", [128, KD * T], BF16, AFB)
        self.att_all = sb("att_all", [128, KD * T], BF16, AFB)
        self.wcur = sb("wcur", [128, KD * SEG], BF16, A1)
        self.wnext = sb("wnext", [128, KD * SEG], BF16, A1 + 16384)
        self.q_fm = sb("q_fm", [128, KD * T], BF16, A1)
        self.wdn = sb("wdn", [128, FC * D], BF16, A2)
        self.k_fm = sb("k_fm", [128, KD * T], BF16, A2)
        self.wo_res = sb("wo_res", [128, KD * D], BF16, A1)
        self.httm2 = sb("httm2", [128, D], BF16, A3)
        self.lnq = sb("lnq", [128, D], F32, A3 + 8192)
        self.actp = sb("actp", [128, FC * T], BF16, A3)
        self.sg = sb("sg", [128, FC * SEG], BF16, A3 + 16384)
        self.wbc = sb("wbc", [128, T], F32, A3 + 24576)
        self.wbc2 = sb("wbc2", [128, T], F32, A3 + 28672)
        self.httm = sb("httm", [128, D], BF16, A3 + 28672)
        self.v_tm = sb("v_tm", [128, KD * T], BF16, A3)
        self.r = sb("r_s", [128, NT * D], F32, RR)
        self.wqkv = [sb("wqkv0", [128, KD * SEG], BF16, RR),
                     sb("wqkv1", [128, KD * SEG], BF16, RR + 16384)]
        self.maskt = sb("maskt_s", [128, 4 * SEG], BF16, RR)
        self.keep2 = sb("keep2", [128, H * 128], BF16, RR + 32768)
        self.mem16b = sb("mem16b", [128, H * 128], BF16, RR + 36864)
        self.slab = [sb("slab0", [128, H * 256], BF16, RR + 8192),
                     sb("slab1", [128, H * 256], BF16, RR + 16384)]
        self.mem16 = sb("mem16", [128, H * 128], BF16, RR + 40960)
        self.fsc = sb("fsc", [128, 4 * SEG], F32, RR + 45056)
        self.rdc = sb("rdc", [128, 16], F32, RR + 64576)
        self.expT3 = sb("expT3", [128, 4 * SEG], BF16, RR + 4096)
        self.expT = sb("expT", [128, 4 * SEG], BF16, RR + 53312)
        self.expT2 = sb("expT2", [128, 4 * SEG], BF16, RR + 57408)
        self.attTs = [sb("attTs0", [128, SEG], BF16, RR + 61504),
                      sb("attTs1", [128, SEG], BF16, RR + 62528)]
        self.attC = sb("attC", [128, SEG], BF16, RR + 63552)
        self.skf2 = [sb("skf2a", [128, 1024], BF16, RR + 32768),
                     sb("skf2b", [128, 1024], BF16, RR + 34816)]
        self.skt2 = [sb("skt2a", [128, 1024], BF16, RR + 36864),
                     sb("skt2b", [128, 1024], BF16, RR + 38912)]
        self.sks2 = [sb("sks2a", [128, 1024], BF16, RR + 40960),
                     sb("sks2b", [128, 1024], BF16, RR + 43008)]
        self.gstg = [sb("gstga", [128, 512], BF16, RR + 57408),
                     sb("gstgb", [128, 512], BF16, RR + 58432)]

        self.sems(["io", "sc", "wg", "cc", "pe", "dv", "ac", "st",
                   "s0", "s1", "s2", "s3", "g0", "g1", "w0", "w1", "w2", "w3"])
        self.sg_war = None
        self.down_war = None
        self.qkv_fill = None
        self.ckq, self.ckv = {}, {}
        self.pM_free = None
        self.cpt_free = None
        self.gbst_war = None

        with nc.Block() as block:
            self.blk = block
            self.emit_prelude()
            if "ffn1" in parts:
                self.emit_ffn(0)
            if "attn" in parts:
                self.emit_attn()
            if "ffn2" in parts:
                self.emit_ffn(1)
            self.emit_store()
        self.es.close()
        return nc

    def emit_prelude(self):
        b, blk = self, self.blk

        @blk.sync
        def _(sy):
            for i in range(NT):
                b.inc(sy.dma_start(out=b.r[:, i * D:(i + 1) * D],
                                   in_=b.x_d[i * 128:(i + 1) * 128, :]), "io", 16)
            for dst, src in [(b.eyeb, b.eyeb_d), (b.eyef, b.eyef_d),
                             (b.o128, b.o128_d), (b.o1, b.o1_d),
                             (b.sel8, b.sel8_d), (b.shiftc, b.shiftc_d),
                             (b.betin, b.betas_d), (b.zmask, b.zmask_d),
                             (b.rmask, b.rmask_d)]:
                b.inc(sy.dma_start(out=dst[:, :], in_=src[:, :]), "io", 16)
            b.inc(sy.dma_start(out=b.rmsc[:, :], in_=b.rmsw_d[:, :]),
                  "io", 16)

        @blk.gpsimd
        def _(g):
            b.inc(g.dma_start(out=b.routf[:, :], in_=b.router_d[:, :]),
                  "wg", 16)
        self.prelude_io = self.cur("io")
        self.prelude_wg = self.cur("wg")

    def emit_norm(self, dst_fm, rms, pt):
        b, blk = self, self.blk
        bar_dv = b.cur("dv")
        bar_ac = b.cur("ac")
        bar_pe = b.cur("pe")
        st = b.st8

        @blk.vector
        def _(v):
            b.wait(v, "vector", b.prelude_io)
            b.wait(v, "vector", bar_ac)
            for i in range(NT):
                ins = v.tensor_reduce(st[:, i:i + 1], b.r[:, i * D:(i + 1) * D],
                                      axis=AX.X, op=ALU.add, negate=True)
            b.inc(ins, "dv")
        red_tok = b.cur("dv")

        @blk.scalar
        def _(sc):
            b.wait(sc, "scalar", b.prelude_io)
            b.wait(sc, "scalar", bar_dv)
            b.wait(sc, "scalar", bar_pe)
            for i in range(NT):
                ins = sc.activation(b.lnq[:, :], b.r[:, i * D:(i + 1) * D],
                                    AF.Square, accum_out=st[:, 16 + i:17 + i])
            b.inc(ins, "ac")
        sq_tok = b.cur("ac")

        c0b, c2b = st[:, 0:8], st[:, 16:24]
        c1b, c3b, c4b, c5b, c6b, c7b = (st[:, 8:16], st[:, 24:32],
                                        st[:, 32:40], st[:, 40:48],
                                        st[:, 48:56], st[:, 56:64])

        @blk.vector
        def _(v):
            b.wait(v, "vector", red_tok)
            b.wait(v, "vector", sq_tok)
            b.chain(v, "vector", "dv",
                    v.tensor_scalar(c1b, c0b, 1.0 / D, None, op0=ALU.mult))
            b.chain(v, "vector", "dv",
                    v.tensor_tensor(c3b, c1b, c1b, op=ALU.mult))
            b.chain(v, "vector", "dv",
                    v.tensor_scalar(c4b, c2b, 1.0 / D, None, op0=ALU.mult))
            b.chain(v, "vector", "dv",
                    v.tensor_tensor(c4b, c4b, c3b, op=ALU.subtract))
            b.inc(v.tensor_scalar(c5b, c4b, LNEPS, None, op0=ALU.add), "dv")

        @blk.scalar
        def _(sc):
            b.wait(sc, "scalar", b.cur("dv"))
            b.inc(sc.activation(c5b, c5b, AF.Sqrt), "ac")

        @blk.vector
        def _(v, rms=rms):
            b.wait(v, "vector", b.cur("ac"))
            b.chain(v, "vector", "dv", v.reciprocal(c5b, c5b))
            if rms:
                b.chain(v, "vector", "dv",
                        v.tensor_tensor(c6b, c5b, c5b, op=ALU.mult))
                b.chain(v, "vector", "dv",
                        v.tensor_tensor(c6b, c6b, c4b, op=ALU.mult))
                b.inc(v.tensor_scalar(c6b, c6b, RMSEPS, None, op0=ALU.add),
                      "dv")

        if rms:
            @blk.scalar
            def _(sc):
                b.wait(sc, "scalar", b.cur("dv"))
                b.inc(sc.activation(c6b, c6b, AF.Sqrt), "ac")

            @blk.vector
            def _(v):
                b.wait(v, "vector", b.cur("ac"))
                b.chain(v, "vector", "dv", v.reciprocal(c6b, c6b))
                b.chain(v, "vector", "dv",
                        v.tensor_tensor(c5b, c5b, c6b, op=ALU.mult))

        @blk.vector
        def _(v):
            b.chain(v, "vector", "dv",
                    v.tensor_tensor(c3b, c1b, c5b, op=ALU.mult))
        stats_tok = b.cur("dv")

        htb = [b.httm, b.httm2]
        ht_done = {}
        for i in range(NT):
            @blk.vector
            def _(v, i=i):
                b.wait(v, "vector", stats_tok)
                if i >= 2:
                    b.wait(v, "vector", ht_done[i - 2])
                if i == 0:
                    b.wait(v, "vector", bar_pe)
                b.inc(v.tensor_scalar(htb[i % 2][:, :], b.r[:, i * D:(i + 1) * D],
                                      c5b[:, i:i + 1], c3b[:, i:i + 1],
                                      op0=ALU.mult, op1=ALU.add), "dv")
            hdv = b.cur("dv")

            @blk.tensor
            def _(t, i=i, hdv=hdv):
                b.wait(t, "tensor", hdv)
                for kc in range(KD):
                    pp = pt[kc % 2]
                    b.wait(t, "tensor", self.pt_free.get(kc % 2))
                    self.pt_ready[kc % 2] = b.inc(
                        t.transpose(pp[:, 0:128],
                                    htb[i % 2][:, kc * 128:(kc + 1) * 128],
                                    b.eyeb[:, :]), "pe")

                    @blk.vector
                    def _(v, i=i, kc=kc, pp=pp):
                        b.wait(v, "vector", self.pt_ready[kc % 2])
                        col = kc * T + i * 128
                        if rms:
                            ins = v.tensor_scalar(dst_fm[:, col:col + 128],
                                                  pp[:, 0:128],
                                                  b.rmsc[:, kc:kc + 1], None,
                                                  op0=ALU.mult)
                        else:
                            ins = v.tensor_copy(dst_fm[:, col:col + 128],
                                                pp[:, 0:128])
                        self.pt_free[kc % 2] = b.inc(ins, "dv")
            ht_done[i] = self.cur("pe")
        return b.cur("dv")

    def emit_ffn(self, call):
        b, blk, nc = self, self.blk, self.nc
        pes = ExitStack()
        pt = [pes.enter_context(nc.psum_tensor(f"pt{call}_{j}", [128, 1024], BF16))
              for j in range(2)]
        pl = [pes.enter_context(nc.psum_tensor(f"pl{call}_{j}", [128, 512], F32))
              for j in range(2)]
        self.pt_ready, self.pt_free = {}, {}

        ht_tok = self.emit_norm(b.act_fm, rms=True, pt=pt)
        if DEV_FFN_STAGE == "norm":
            if DEV_DUMP:
                self.emit_dump()
            pes.close()
            return

        @blk.tensor
        def _(t):
            b.wait(t, "tensor", ht_tok)
            b.wait(t, "tensor", b.prelude_wg)
            b.wait(t, "tensor", b.cur("dv"))
            for tt in range(2):
                for kk in range(KD):
                    ins = t.matmul(pl[tt][0:8, :], b.routf[:, kk * E:(kk + 1) * E],
                             b.act_fm[:, kk * T + tt * 512: kk * T + (tt + 1) * 512],
                             start=(kk == 0), stop=(kk == KD - 1))
            b.inc(ins, "pe")
        lg_pe = b.cur("pe")

        @blk.vector
        def _(v):
            b.wait(v, "vector", lg_pe)
            v.tensor_copy(b.logit[:, 0:512], pl[0][0:8, :])
            b.inc(v.tensor_copy(b.logit[:, 512:1024], pl[1][0:8, :]), "dv")
        lg_dv = b.cur("dv")
        if DEV_FFN_STAGE == "logits":
            pes.close()
            return
        tp_free = {0: None, 1: None}

        for sl in range(NT):
            @blk.tensor
            def _(t, sl=sl):
                b.wait(t, "tensor", lg_dv)
                b.wait(t, "tensor", tp_free[sl % 2])
                b.inc(t.transpose(pl[sl % 2][:, 0:8],
                                  b.logit[:, sl * 128:(sl + 1) * 128],
                                  b.eyef[0:8, 0:8]), "pe")
            tpr = b.cur("pe")

            @blk.vector
            def _(v, sl=sl, tpr=tpr):
                b.wait(v, "vector", tpr)
                lt, pr = b.rsc[:, 0:8], b.rsc[:, 8:16]
                m = b.st8[:, 56:57]
                v.tensor_copy(lt, pl[sl % 2][:, 0:8])
                tok = b.inc(v.tensor_reduce(
                    m, lt, axis=AX.X, op=ALU.max, negate=True), "dv")
                tp_free[sl % 2] = tok
                b.wait(v, "vector", tok)
                b.chain(v, "vector", "dv",
                        v.tensor_scalar(pr, lt, m, None, op0=ALU.add))

            @blk.scalar
            def _(sc):
                b.wait(sc, "scalar", b.cur("dv"))
                b.inc(sc.activation(b.rsc[:, 8:16], b.rsc[:, 8:16], AF.Exp,
                                    accum_out=b.st8[:, 57:58]), "ac")

            if DEV_FFN_STAGE == "sfm1":
                continue

            @blk.vector
            def _(v, sl=sl):
                b.wait(v, "vector", b.cur("ac"))
                pr, t1, t2 = b.rsc[:, 8:16], b.rsc[:, 16:24], b.rsc[:, 24:32]
                m = b.st8[:, 58:59]
                b.chain(v, "vector", "dv", v.reciprocal(m, b.st8[:, 57:58]))
                b.chain(v, "vector", "dv",
                        v.tensor_scalar(pr, pr, m, None, op0=ALU.mult))
                b.chain(v, "vector", "dv",
                        v.tensor_reduce(m, pr, axis=AX.X, op=ALU.max))
                b.chain(v, "vector", "dv",
                        v.tensor_scalar(t1, pr, m, None, op0=ALU.is_ge))
                b.chain(v, "vector", "dv",
                        v.tensor_tensor(t2, pr, t1, op=ALU.mult))
                b.chain(v, "vector", "dv",
                        v.tensor_tensor(t2, pr, t2, op=ALU.subtract))
                b.chain(v, "vector", "dv",
                        v.tensor_reduce(m, t2, axis=AX.X, op=ALU.max))
                b.chain(v, "vector", "dv",
                        v.tensor_scalar(t1, pr, m, None, op0=ALU.is_ge))
                b.chain(v, "vector", "dv",
                        v.tensor_tensor(t2, pr, t1, op=ALU.mult))
                b.chain(v, "vector", "dv",
                        v.tensor_reduce(m, t2, axis=AX.X, op=ALU.add))
                b.chain(v, "vector", "dv", v.reciprocal(m, m))
                wsl = b.wde[:, sl * E:(sl + 1) * E]
                b.chain(v, "vector", "dv",
                        v.tensor_scalar(wsl, t2, m, None, op0=ALU.mult))
                b.inc(v.tensor_copy(b.wde16[:, sl * E:(sl + 1) * E], wsl), "dv")

            @blk.tensor
            def _(t, sl=sl):
                b.wait(t, "tensor", b.cur("dv"))
                b.wait(t, "tensor", self.pt_free.get(0))
                b.inc(t.transpose(pt[0][0:8, 0:128],
                                  b.wde16[:, sl * E:(sl + 1) * E],
                                  b.eyeb[:, :]), "pe")

            @blk.vector
            def _(v, sl=sl):
                b.wait(v, "vector", b.cur("pe"))
                self.pt_free[0] = b.inc(
                    v.tensor_copy(b.wT[:, sl * 128:(sl + 1) * 128],
                                  pt[0][0:8, 0:128]), "dv")
        wT_tok = b.cur("dv")
        pes.close()
        if DEV_FFN_STAGE in ("router", "sfm1"):
            if DEV_DUMP:
                self.emit_dump()
            return

        pes2 = ExitStack()
        pb = [pes2.enter_context(nc.psum_tensor(f"pb{call}_{j}", [128, 512], F32))
              for j in range(8)]
        pb_free = {j: None for j in range(8)}

        slot_t = [b.wcur, b.wnext]
        slot_sem = ["s0", "s1"]
        slot_cons = [b.cur("pe"), b.cur("pe")]
        fills = []
        dwn_fills = []
        self.wdn_cons = b.cur("pe")

        def stream_half(g, idx, src2d):
            slot = idx % 2
            b.wait(g, "gpsimd", slot_cons[slot])
            tok = b.inc(g.dma_start(out=slot_t[slot][:, :], in_=src2d),
                        slot_sem[slot], 16)
            fills.append(tok)

        def half_src(w2d, half):
            return w2d[:, half * (KD * 512):(half + 1) * (KD * 512)]

        exps = [(self.wg_d[e], self.wu_d[e], self.wd_d[e], False)
                for e in range(E)] + [(self.swg_d, self.swu_d, self.swd_d, True)]

        if DEV_NEXP is not None:
            exps = exps[:DEV_NEXP]
        halves = []
        for (gd, ud, dd, sh) in exps:
            halves += [half_src(gd, 0), half_src(ud, 0),
                       half_src(gd, 1), half_src(ud, 1)]

        def wdown_stream(g, e):
            b.wait(g, "gpsimd", self.wdn_cons)
            tok = b.inc(g.dma_start(out=b.wdn[:, :], in_=exps[e][2][:, :]),
                        "s2", 16)
            dwn_fills.append(tok)

        @blk.gpsimd
        def _(g):
            stream_half(g, 0, halves[0])
            stream_half(g, 1, halves[1])
            wdown_stream(g, 0)
        hidx = 2

        wbc_buf = [b.wbc, b.wbc2]
        sg_war = {j: None for j in range(8)}
        sgrdy = {j: None for j in range(8)}

        def emit_wbc(e):
            @blk.tensor
            def _(t, e=e):
                b.wait(t, "tensor", wT_tok)
                b.wait(t, "tensor", pb_free[6])
                b.wait(t, "tensor", pb_free[7])
                t.matmul(pb[6][:, :], b.sel8[:, e * 128:(e + 1) * 128],
                         b.wT[0:8, 0:512], start=True, stop=True)
                ins = t.matmul(pb[7][:, :], b.sel8[:, e * 128:(e + 1) * 128],
                               b.wT[0:8, 512:1024], start=True, stop=True)
                b.inc(ins, "pe")

            @blk.vector
            def _(v, e=e):
                b.wait(v, "vector", b.cur("pe"))
                v.tensor_copy(wbc_buf[e % 2][:, 0:512], pb[6][:, :])
                tok = b.inc(v.tensor_copy(wbc_buf[e % 2][:, 512:1024],
                                          pb[7][:, :]), "dv")
                pb_free[6] = tok
                pb_free[7] = tok

        emit_wbc(0)
        for ei, (gd, ud, dd, sh) in enumerate(exps):
            for hp in range(4):
                gidx = ei * 4 + hp
                isg = (hp % 2 == 0)
                half = hp // 2
                slot = gidx % 2
                fill_tok = fills[gidx]
                for f in range(4):
                    for tt in range(2):
                        j = f * 2 + tt

                        @blk.tensor
                        def _(t, f=f, tt=tt, j=j, slot=slot, fill_tok=fill_tok):
                            b.wait(t, "tensor", fill_tok)
                            b.wait(t, "tensor", ht_tok)
                            b.wait(t, "tensor", pb_free[j])
                            sl = slot_t[slot]
                            for kk in range(KD):
                                ins = t.matmul(
                                    pb[j][:, :],
                                    sl[:, kk * 512 + f * 128: kk * 512 + (f + 1) * 128],
                                    b.act_fm[:, kk * T + tt * 512: kk * T + (tt + 1) * 512],
                                    start=(kk == 0), stop=(kk == KD - 1))
                            b.inc(ins, "pe")
                        bank_pe = b.cur("pe")
                        if isg:
                            @blk.scalar
                            def _(sc, j=j, bank_pe=bank_pe):
                                b.wait(sc, "scalar", bank_pe)
                                b.wait(sc, "scalar", sg_war[j])
                                b.inc(sc.activation(b.sg[:, j * 512:(j + 1) * 512],
                                                    pb[j][:, :], AF.Silu), "ac")
                            sgtok = b.cur("ac")
                            pb_free[j] = sgtok
                            if not sh:
                                @blk.vector
                                def _(v, j=j, tt=tt, ei=ei, sgtok=sgtok):
                                    b.wait(v, "vector", sgtok)
                                    b.inc(v.tensor_tensor(
                                        b.sg[:, j * 512:(j + 1) * 512],
                                        b.sg[:, j * 512:(j + 1) * 512],
                                        wbc_buf[ei % 2][:, tt * 512:(tt + 1) * 512],
                                        op=ALU.mult), "dv")
                                sgrdy[j] = b.cur("dv")
                            else:
                                sgrdy[j] = sgtok
                        else:
                            @blk.vector
                            def _(v, j=j, f=f, tt=tt, half=half, bank_pe=bank_pe):
                                b.wait(v, "vector", bank_pe)
                                b.wait(v, "vector", self.down_war)
                                b.wait(v, "vector", sgrdy[j])
                                col = (half * 4 + f) * T + tt * 512
                                ins = v.tensor_tensor(
                                    b.actp[:, col:col + 512],
                                    b.sg[:, j * 512:(j + 1) * 512],
                                    pb[j][:, :], op=ALU.mult)
                                tok = b.inc(ins, "dv")
                                pb_free[j] = tok
                                sg_war[j] = tok

                slot_cons[slot] = b.cur("pe")
                if hidx < len(halves):
                    @blk.gpsimd
                    def _(g, hidx=hidx):
                        stream_half(g, hidx, halves[hidx])
                    hidx += 1
                if hp == 1 and ei + 1 < len(exps) and not exps[ei + 1][3]:
                    emit_wbc(ei + 1)

            act_tok = b.cur("dv")

            @blk.tensor
            def _(t, ei=ei, act_tok=act_tok):
                b.wait(t, "tensor", dwn_fills[ei])
                b.wait(t, "tensor", act_tok)
                for ts in range(NT):
                    grp = 4 * (ts % 2)
                    for dsl in range(4):
                        b.wait(t, "tensor", pb_free[grp + dsl])
                    for kf in range(FC):
                        for dsl in range(4):
                            ins = t.matmul(pb[grp + dsl][:, :],
                                     b.actp[:, kf * T + ts * 128: kf * T + (ts + 1) * 128],
                                     b.wdn[:, kf * D + dsl * 512: kf * D + (dsl + 1) * 512],
                                     start=(kf == 0), stop=(kf == FC - 1))
                    tok = b.inc(ins, "pe")

                    @blk.vector
                    def _(v, ts=ts, tok=tok, grp=grp):
                        b.wait(v, "vector", tok)
                        for dsl in range(4):
                            ins = v.scalar_tensor_tensor(
                                b.r[:, ts * D + dsl * 512: ts * D + (dsl + 1) * 512],
                                pb[grp + dsl][:, :], 0.5,
                                b.r[:, ts * D + dsl * 512: ts * D + (dsl + 1) * 512],
                                op0=ALU.mult, op1=ALU.add)
                        etok = b.inc(ins, "dv")
                        for dsl in range(4):
                            pb_free[grp + dsl] = etok
            self.wdn_cons = b.cur("pe")
            self.down_war = self.wdn_cons
            if DEV_DUMP and ei == len(exps) - 1:
                self.emit_dump()
            if ei + 1 < len(exps):
                @blk.gpsimd
                def _(g, ei=ei):
                    wdown_stream(g, ei + 1)
            if call == 0 and ei < 8:
                @blk.gpsimd
                def _(g, ei=ei):
                    qtr = KD * 512
                    if ei < 4:
                        self.ckq[ei] = b.inc(g.dma_start(
                            out=self.wqb_d[:, ei * qtr:(ei + 1) * qtr],
                            in_=self.wq_d[:, ei * qtr:(ei + 1) * qtr]),
                            "wg", 16)
                    else:
                        self.ckv[ei - 4] = b.inc(g.dma_start(
                            out=self.wvb_d[:, (ei - 4) * qtr:(ei - 3) * qtr],
                            in_=self.wv_d[:, (ei - 4) * qtr:(ei - 3) * qtr]),
                            "wg", 16)
        pes2.close()

    def emit_dump(self):
        b, blk = self, self.blk

        @blk.gpsimd
        def _(g):
            b.wait(g, "gpsimd", b.cur("dv"))
            b.wait(g, "gpsimd", b.cur("ac"))
            b.wait(g, "gpsimd", b.cur("pe"))
            b.inc(g.dma_start(out=self.dbg_actp, in_=b.actp[:, :]), "io", 16)
            b.inc(g.dma_start(out=self.dbg_ht, in_=b.act_fm[:, :]), "io", 16)
            b.inc(g.dma_start(out=self.dbg_sg, in_=b.sg[:, :]), "io", 16)
            b.inc(g.dma_start(out=self.dbg_wbc, in_=b.wbc[:, :]), "io", 16)
            b.inc(g.dma_start(out=self.dbg_wt, in_=b.wT[:, :]), "io", 16)
            b.inc(g.dma_start(out=self.dbg_httm, in_=b.httm[:, :]), "io", 16)
            b.inc(g.dma_start(out=self.dbg_st8, in_=b.st8[:, :]), "io", 16)
            g.wait_ge(b.sem_h["io"], b.sem_n["io"])

    def emit_attn(self):
        b, blk, nc = self, self.blk, self.nc
        pes = ExitStack()
        pt = [pes.enter_context(nc.psum_tensor(f"apt{j}", [128, 1024], BF16))
              for j in range(2)]
        plb = pes.enter_context(nc.psum_tensor("plb", [128, 512], F32))
        self.pt_ready, self.pt_free = {}, {}

        @blk.scalar
        def _(sc):
            b.wait(sc, "scalar", b.prelude_io)
            b.inc(sc.activation(b.betin[:, :], b.betin[:, :], AF.Sigmoid), "ac")

        @blk.tensor
        def _(t):
            b.wait(t, "tensor", b.cur("ac"))
            b.wait(t, "tensor", b.cur("dv"))
            b.inc(t.transpose(plb[:, 0:16], b.betin[:, :],
                              b.eyef[0:16, 0:16]), "pe")

        @blk.vector
        def _(v):
            b.wait(v, "vector", b.cur("pe"))
            b.chain(v, "vector", "dv", v.tensor_copy(b.beta[:, :], plb[:, 0:16]))
            b.inc(v.tensor_scalar(b.beta1m[:, :], b.beta[:, :], -1.0, 1.0,
                                  op0=ALU.mult, op1=ALU.add), "dv")

        if not self.ckq:
            @blk.gpsimd
            def _(g):
                qtr = KD * 512
                for grp in range(4):
                    self.ckq[grp] = b.inc(g.dma_start(
                        out=self.wqb_d[:, grp * qtr:(grp + 1) * qtr],
                        in_=self.wq_d[:, grp * qtr:(grp + 1) * qtr]),
                        "wg", 16)
                    self.ckv[grp] = b.inc(g.dma_start(
                        out=self.wvb_d[:, grp * qtr:(grp + 1) * qtr],
                        in_=self.wv_d[:, grp * qtr:(grp + 1) * qtr]),
                        "wg", 16)

        @blk.sync
        def _(sy, ffn1_tok=b.cur("dv")):
            b.wait(sy, "sync", ffn1_tok)
            for i in range(4):
                b.inc(sy.dma_start(out=b.rsp_d[i * 128:(i + 1) * 128, :],
                                   in_=b.r[:, i * D:(i + 1) * D]), "io", 16)
        spill_tok = b.cur("io")

        @blk.scalar
        def _(sc, ffn1_tok=b.cur("dv")):
            b.wait(sc, "scalar", ffn1_tok)
            for i in range(4, NT):
                b.inc(sc.dma_start(out=b.rsp_d[i * 128:(i + 1) * 128, :],
                                   in_=b.r[:, i * D:(i + 1) * D]), "sc", 16)
        spill_tok2 = b.cur("sc")

        xln_tok = self.emit_norm(b.act_fm, rms=False, pt=pt)
        pes.close()

        pes2 = ExitStack()
        pb = [pes2.enter_context(nc.psum_tensor(f"qpb{j}", [128, 512], F32))
              for j in range(8)]
        pb_free = {j: None for j in range(8)}
        slot_sem = ["s0", "s1"]
        qkv_cons = [b.cur("pe"), b.cur("pe")]
        wlist = [(self.wkb_d, b.k_fm), (self.wvb_d, b.v_tm),
                 (self.wqb_d, b.q_fm)]

        def qkv_group(gi, pb, pb_free):
            wi, grp = gi // 4, gi % 4
            slot = gi % 2
            wd, dst = wlist[wi]
            cast_tok = (None if wi == 0 else
                        (self.ckv[grp] if wi == 1 else self.ckq[grp]))

            if wi == 0:
                @blk.gpsimd
                def _(g, slot=slot, grp=grp):
                    b.wait(g, "gpsimd", spill_tok)
                    b.wait(g, "gpsimd", spill_tok2)
                    b.wait(g, "gpsimd", qkv_cons[slot])
                    self.qkv_fill = b.inc(g.dma_start(
                        out=b.wqkv[slot][:, :],
                        in_=self.wk_d[:, grp * (KD * 512):(grp + 1) * (KD * 512)]),
                        slot_sem[slot], 16)
                fill_tok = self.qkv_fill
            elif slot == 0:
                @blk.sync
                def _(sy, slot=slot, wd=wd, grp=grp, cast_tok=cast_tok):
                    b.wait(sy, "sync", spill_tok)
                    b.wait(sy, "sync", spill_tok2)
                    b.wait(sy, "sync", cast_tok)
                    b.wait(sy, "sync", qkv_cons[slot])
                    self.qkv_fill = b.inc(sy.dma_start(
                        out=b.wqkv[slot][:, :],
                        in_=wd[:, grp * (KD * 512):(grp + 1) * (KD * 512)]),
                        slot_sem[slot], 16)
                fill_tok = self.qkv_fill
            else:
                @blk.scalar
                def _(sc, slot=slot, wd=wd, grp=grp, cast_tok=cast_tok):
                    b.wait(sc, "scalar", spill_tok)
                    b.wait(sc, "scalar", spill_tok2)
                    b.wait(sc, "scalar", cast_tok)
                    b.wait(sc, "scalar", qkv_cons[slot])
                    self.qkv_fill = b.inc(sc.dma_start(
                        out=b.wqkv[slot][:, :],
                        in_=wd[:, grp * (KD * 512):(grp + 1) * (KD * 512)]),
                        slot_sem[slot], 16)
                fill_tok = self.qkv_fill

            if wi == 1:
                @blk.tensor
                def _(t, slot=slot, grp=grp, fill_tok=fill_tok):
                    b.wait(t, "tensor", fill_tok)
                    b.wait(t, "tensor", xln_tok)
                    b.wait(t, "tensor", self.qkv_bar_dv)
                    b.wait(t, "tensor", self.qkv_bar_ac)
                    for ts in range(NT):
                        bk = ts % len(pb)
                        b.wait(t, "tensor", pb_free[bk])
                        for kk in range(KD):
                            ins = t.matmul(pb[bk][:, :],
                                     b.act_fm[:, kk * T + ts * 128: kk * T + (ts + 1) * 128],
                                     b.wqkv[slot][:, kk * 512:(kk + 1) * 512],
                                     start=(kk == 0), stop=(kk == KD - 1))
                        tok = b.inc(ins, "pe")
                        qkv_cons[slot] = tok

                        @blk.vector
                        def _(v, ts=ts, bk=bk, grp=grp, tok=tok):
                            b.wait(v, "vector", tok)
                            ins = v.tensor_copy(
                                b.v_tm[:, ts * D + grp * 512: ts * D + (grp + 1) * 512],
                                pb[bk][:, :])
                            pb_free[bk] = b.inc(ins, "dv")
                return qkv_cons[slot]

            @blk.tensor
            def _(t, slot=slot, grp=grp, dst=dst, fill_tok=fill_tok):
                b.wait(t, "tensor", fill_tok)
                b.wait(t, "tensor", xln_tok)
                b.wait(t, "tensor", self.qkv_bar_dv)
                b.wait(t, "tensor", self.qkv_bar_ac)
                for j in range(8):
                    b.wait(t, "tensor", pb_free[j])
                for kk in range(KD):
                    for dd in range(4):
                        for tt in range(2):
                            ins = t.matmul(pb[dd * 2 + tt][:, :],
                                     b.wqkv[slot][:, kk * 512 + dd * 128: kk * 512 + (dd + 1) * 128],
                                     b.act_fm[:, kk * T + tt * 512: kk * T + (tt + 1) * 512],
                                     start=(kk == 0), stop=(kk == KD - 1))
                tok = b.inc(ins, "pe")
                qkv_cons[slot] = tok

                @blk.vector
                def _(v, grp=grp, dst=dst, tok=tok):
                    b.wait(v, "vector", tok)
                    for dd in range(4):
                        for tt in range(2):
                            col = (grp * 4 + dd) * T + tt * 512
                            ins = v.tensor_copy(dst[:, col:col + 512],
                                                pb[dd * 2 + tt][:, :])
                    etok = b.inc(ins, "dv")
                    for j in range(8):
                        pb_free[j] = etok
            return qkv_cons[slot]

        q_slot_pe = {}
        self.qfill = {}
        self.qkv_bar_dv = None
        self.qkv_bar_ac = None
        self.k_tok = None
        for gi in range(4):
            qkv_group(gi, pb, pb_free)
        self.k_tok = b.cur("dv")

        for c4 in range(8):
            pairs = [(hs // H, hs % H) for hs in range(c4 * 4, c4 * 4 + 4)]

            @blk.scalar
            def _(sc, pairs=pairs, c4=c4):
                b.wait(sc, "scalar", self.k_tok)
                b.wait(sc, "scalar", spill_tok)
                b.wait(sc, "scalar", spill_tok2)
                for q2, (seg, h) in enumerate(pairs):
                    sc.activation(b.fsc[:, q2 * 512:(q2 + 1) * 512],
                                  b.k_fm[:, h * T + seg * 512:
                                         h * T + (seg + 1) * 512],
                                  AF.Relu, scale=-1.0)
                for q2 in range(4):
                    sc.activation(b.fsc[:, q2 * 512:(q2 + 1) * 512],
                                  b.fsc[:, q2 * 512:(q2 + 1) * 512],
                                  AF.Exp, scale=-1.0,
                                  accum_out=b.zsb[:, 64 + c4 * 4 + q2:
                                                  65 + c4 * 4 + q2])
                for q2, (seg, h) in enumerate(pairs):
                    ins = sc.activation(
                        b.fsc[:, q2 * 512:(q2 + 1) * 512],
                        b.k_fm[:, h * T + seg * 512: h * T + (seg + 1) * 512],
                        AF.Relu,
                        accum_out=b.zsb[:, 96 + c4 * 4 + q2:
                                        97 + c4 * 4 + q2])
                b.inc(ins, "ac")
        zinc_ac = b.cur("ac")

        @blk.vector
        def _(v):
            b.wait(v, "vector", zinc_ac)
            b.inc(v.tensor_tensor(b.zsb[:, 0:2 * H], b.zsb[:, 64:96],
                                  b.zsb[:, 96:128], op=ALU.add), "dv")
        zs_tok = b.cur("dv")

        @blk.scalar
        def _(sc):
            b.wait(sc, "scalar", zs_tok)
            b.inc(sc.dma_start(out=self.cc1i_t.ap(), in_=b.zsb[:, 0:2 * H]),
                  "io", 16)
        cc1_st = b.cur("io")

        self.qh_tok = {}
        for gi in (8, 9):
            tokpe = qkv_group(gi, pb, pb_free)
            q_slot_pe[gi % 2] = tokpe
            self.qh_tok[gi - 8] = b.cur("dv")

        @blk.gpsimd
        def _(g):
            b.wait(g, "gpsimd", cc1_st)
            b.inc(g.collective_compute(
                "AllGather", ALU.bypass, replica_groups=[list(range(NCORES))],
                ins=[self.cc1i_t.ap().opt()], outs=[self.cc1o_t.ap().opt()]),
                "cc")
            g.wait_ge(b.sem_h["cc"], b.sem_n["cc"])
            b.inc(g.dma_start(
                out=b.zsb.rearrange("p (r c) -> p r c", r=NCORES),
                in_=self.cc1o_t.ap().rearrange("(r p) c -> p r c", p=128)),
                "sc", 16)
        cc1_tok = b.cur("cc")
        zsb_tok = b.cur("sc")

        for gi in (10, 11):
            tokpe = qkv_group(gi, pb, pb_free)
            q_slot_pe[gi % 2] = tokpe
            self.qh_tok[gi - 8] = b.cur("dv")
        q_tok = b.cur("dv")

        @blk.vector
        def _(v):
            b.wait(v, "vector", zsb_tok)
            b.chain(v, "vector", "dv", v.memset(b.z_a[:, :], 1.0 / DK))
            b.chain(v, "vector", "dv", v.memset(b.z_b[:, :], 1.0 / DK))
            for j in range(NSEG):
                sl = b.zsb[:, j * H:(j + 1) * H]
                b.chain(v, "vector", "dv",
                        v.scalar_tensor_tensor(b.z_a[:, :], sl,
                                               b.zmask[:, j:j + 1],
                                               b.z_a[:, :],
                                               op0=ALU.mult, op1=ALU.add))
                b.chain(v, "vector", "dv",
                        v.scalar_tensor_tensor(b.z_b[:, :], sl,
                                               b.zmask[:, NSEG + j:NSEG + j + 1],
                                               b.z_b[:, :],
                                               op0=ALU.mult, op1=ALU.add))
            v.tensor_copy(b.z16[:, 0:H], b.z_a[:, :])
            b.inc(v.tensor_copy(b.z16[:, H:2 * H], b.z_b[:, :]), "dv")
        z_tok = b.cur("dv")

        pes2.close()
        self.qkv_bar_dv = q_tok
        pes3 = ExitStack()
        gsig = [pes3.enter_context(nc.psum_tensor(f"gsig{j}", [128, 1024], BF16))
                for j in range(2)]
        gdd = pes3.enter_context(nc.psum_tensor("gdd", [128, 512], F32))
        ggb = [pes3.enter_context(nc.psum_tensor(f"ggb{j}", [128, 512], F32))
               for j in range(2)]
        pes2v = ExitStack()
        pbv = [pes2v.enter_context(nc.psum_tensor(f"vpb{j}", [128, 512], F32))
               for j in range(3)]
        pbv_free = {j: None for j in range(3)}
        tr_free = {0: None, 1: None}
        gdd_free = [None]
        ggb_free = {0: None, 1: None}
        gstg_war = {0: None, 1: None}
        prev_dmm = {0: None, 1: None}

        def gb_sigma(seg, h0, par):
            kseg = [b.k_fm[:, (h0 + hh) * T + seg * 512:
                           (h0 + hh) * T + (seg + 1) * 512] for hh in range(2)]

            @blk.vector
            def _(v, kseg=kseg, par=par):
                b.wait(v, "vector", z_tok)
                b.wait(v, "vector", prev_dmm[par])
                for hh in range(2):
                    ins = v.tensor_scalar(b.fsc[:, hh * 512:(hh + 1) * 512],
                                          kseg[hh], 0.0, None, op0=ALU.min)
                b.inc(ins, "dv")

            @blk.scalar
            def _(sc):
                b.wait(sc, "scalar", b.cur("dv"))
                b.inc(sc.activation(b.fsc[:, 0:1024], b.fsc[:, 0:1024],
                                    AF.Exp), "ac")

            @blk.vector
            def _(v, kseg=kseg, par=par):
                b.wait(v, "vector", b.cur("ac"))
                for hh in range(2):
                    v.tensor_scalar(b.fsc[:, 1024 + hh * 512:1024 + (hh + 1) * 512],
                                    kseg[hh], 0.0, None, op0=ALU.max)
                for hh in range(2):
                    ins = v.tensor_tensor(
                        b.skf2[par][:, hh * 512:(hh + 1) * 512],
                        b.fsc[:, hh * 512:(hh + 1) * 512],
                        b.fsc[:, 1024 + hh * 512:1024 + (hh + 1) * 512],
                        op=ALU.add)
                b.inc(ins, "dv")
            return b.cur("dv")

        def gb_pe(seg, h0, par, vg, sk_tok):
            @blk.tensor
            def _(t, par=par, sk_tok=sk_tok, seg=seg, h0=h0, vg=vg):
                b.wait(t, "tensor", sk_tok)
                b.wait(t, "tensor", vg)
                b.wait(t, "tensor", tr_free[par])
                b.wait(t, "tensor", gdd_free[0])
                for hh in range(2):
                    for j in range(4):
                        t.transpose(gsig[par][:, (hh * 4 + j) * 128:(hh * 4 + j + 1) * 128],
                                    b.skf2[par][:, hh * 512 + j * 128: hh * 512 + (j + 1) * 128],
                                    b.eyeb[:, :])
                for hh in range(2):
                    zcol = b.z16[:, seg * H + h0 + hh: seg * H + h0 + hh + 1]
                    for j in range(4):
                        ins = t.matmul(gdd[:, hh * 4 + j: hh * 4 + j + 1],
                                       b.skf2[par][:, hh * 512 + j * 128: hh * 512 + (j + 1) * 128],
                                       zcol, start=True, stop=True)
                b.inc(ins, "pe")
            trd_pe = b.cur("pe")
            prev_dmm[par] = trd_pe

            @blk.vector
            def _(v, par=par, trd_pe=trd_pe):
                b.wait(v, "vector", trd_pe)
                b.wait(v, "vector", gstg_war[par])
                tok = b.inc(v.tensor_copy(b.skt2[par][:, :], gsig[par][:, :]),
                            "dv")
                tr_free[par] = tok
                ins = v.reciprocal(b.rdc[:, 0:8], gdd[:, 0:8])
                tok2 = b.inc(ins, "dv")
                b.wait(v, "vector", tok2)
                for q2 in range(8):
                    ins = v.tensor_scalar(
                        b.sks2[par][:, q2 * 128:(q2 + 1) * 128],
                        b.skt2[par][:, q2 * 128:(q2 + 1) * 128],
                        b.rdc[:, q2:q2 + 1], None, op0=ALU.mult)
                tok3 = b.inc(ins, "dv")
            gdd_free[0] = b.cur("dv")
            sc_tok = gdd_free[0]

            @blk.tensor
            def _(t, par=par, sc_tok=sc_tok, seg=seg, h0=h0):
                b.wait(t, "tensor", sc_tok)
                b.wait(t, "tensor", ggb_free[par])
                for hh in range(2):
                    for j in range(4):
                        t.matmul(ggb[par][:, hh * 256: hh * 256 + 128],
                                 b.skt2[par][:, (hh * 4 + j) * 128:(hh * 4 + j + 1) * 128],
                                 b.sks2[par][:, (hh * 4 + j) * 128:(hh * 4 + j + 1) * 128],
                                 start=(j == 0), stop=(j == 3))
                    for j in range(4):
                        vcol = (seg * 4 + j) * D + (h0 + hh) * 128
                        ins = t.matmul(ggb[par][:, hh * 256 + 128: hh * 256 + 256],
                                 b.skt2[par][:, (hh * 4 + j) * 128:(hh * 4 + j + 1) * 128],
                                 b.v_tm[:, vcol:vcol + 128],
                                 start=(j == 0), stop=(j == 3))
                b.inc(ins, "pe")
            gb_pe_tok = b.cur("pe")

            @blk.vector
            def _(v, par=par, gb_pe_tok=gb_pe_tok):
                b.wait(v, "vector", gb_pe_tok)
                for hh in range(2):
                    v.tensor_tensor(b.gstg[par][:, hh * 256: hh * 256 + 128],
                                    b.eyef[:, :],
                                    ggb[par][:, hh * 256: hh * 256 + 128],
                                    op=ALU.subtract)
                    ins = v.tensor_copy(
                        b.gstg[par][:, hh * 256 + 128: hh * 256 + 256],
                        ggb[par][:, hh * 256 + 128: hh * 256 + 256])
                tok = b.inc(ins, "dv")
                ggb_free[par] = tok

            @blk.sync
            def _(sy, par=par, seg=seg, h0=h0):
                b.wait(sy, "sync", b.cur("dv"))
                gstg_war[par] = b.inc(sy.dma_start(
                    out=self.cc2i_t.ap()[:, seg * (H * 256) + h0 * 256:
                                         seg * (H * 256) + (h0 + 2) * 256],
                    in_=b.gstg[par][:, :]), f"g{par}", 16)

        GBs = []
        for grp in range(4):
            for h0, seg in ((4 * grp, 0), (4 * grp, 1),
                            (4 * grp + 2, 0), (4 * grp + 2, 1)):
                GBs.append((seg, h0, grp))
        sk_toks = {}
        vgt = {}
        for grp in range(4):
            qkv_group(4 + grp, pbv, pbv_free)
            vgt[grp] = b.cur("dv")
            for q in range(4):
                G = 4 * grp + q
                if G >= 2:
                    gp = G - 2
                    gb_pe(GBs[gp][0], GBs[gp][1], gp % 2,
                          vgt[GBs[gp][2]], sk_toks[gp])
                sk_toks[G] = gb_sigma(GBs[G][0], GBs[G][1], G % 2)
        for gp in (14, 15):
            gb_pe(GBs[gp][0], GBs[gp][1], gp % 2,
                  vgt[GBs[gp][2]], sk_toks[gp])
        v_tok = b.cur("dv")
        pes2v.close()
        gb_all = b.cur("g0")
        gb_all2 = b.cur("g1")
        pes3.close()

        @blk.gpsimd
        def _(g):
            b.wait(g, "gpsimd", gb_all)
            b.wait(g, "gpsimd", gb_all2)
            b.inc(g.collective_compute(
                "AllGather", ALU.bypass, replica_groups=[list(range(NCORES))],
                ins=[self.cc2i_t.ap().opt()], outs=[self.cc2o_t.ap().opt()]),
                "cc")
        cc2_tok = b.cur("cc")

        @blk.scalar
        def _(sc):
            b.wait(sc, "scalar", qkv_cons[0])
            b.wait(sc, "scalar", qkv_cons[1])
            b.inc(sc.dma_start(out=b.maskt[:, :], in_=b.maskt_d[:, :]), "sc", 16)
        maskt_tok = b.cur("sc")

        pes4 = ExitStack()
        pS = [pes4.enter_context(nc.psum_tensor(f"pS{j}", [128, 512], F32))
              for j in range(3)]
        pAT = pes4.enter_context(nc.psum_tensor("pAT", [128, 512], F32))
        pDT = pes4.enter_context(nc.psum_tensor("pDT", [128, 16], F32))
        cpt = pes4.enter_context(nc.psum_tensor("cpt", [128, 512], BF16))
        pRp = [pes4.enter_context(nc.psum_tensor(f"pRp{j}", [128, 512], F32))
               for j in range(2)]

        @blk.vector
        def _(v):
            b.wait(v, "vector", z_tok)
            b.wait(v, "vector", b.cur("pe"))
            b.inc(v.memset(b.mem16[:, :], 0.0), "dv")
        memz_tok = b.cur("dv")

        ET = [b.expT, b.expT2, b.expT3]
        slab_cons = [q_slot_pe[0], q_slot_pe[1]]
        slab_sems = ["s2", "s3"]
        commitA = {-1: memz_tok}
        commitB = {-1: memz_tok}
        halfB_pe = {}
        exp_done = {0: None, 1: None, 2: None}
        expT_war = {}
        sc_pe = {}
        sq_done = {}
        sqe_ac = {-1: None}
        exp_tok = {}
        ad_pe = {-1: None}
        ev_dv = {-1: None}
        tr_pe = {}
        bl_dv = {-1: None}
        fills = {}

        def emit_fill(s):
            slot = s % 2
            if s < NROUND:
                r_, q_ = s // 2, s % 2
                src = self.cc2o_t.ap()[r_ * 128:(r_ + 1) * 128,
                                       q_ * (H * 256):(q_ + 1) * (H * 256)]
            else:
                src = self.cc2i_t.ap()[:, 0:H * 256]

            @blk.sync
            def _(sy, slot=slot, src=src):
                b.wait(sy, "sync", cc2_tok)
                b.wait(sy, "sync", slab_cons[slot])
                b.inc(sy.dma_start(out=b.slab[slot][:, :], in_=src),
                      slab_sems[slot], 16)
            fills[s] = b.cur(slab_sems[slot])

        def replay_half(s, hb):
            slot = s % 2
            own = (s == NROUND)

            @blk.tensor
            def _(t, s=s, hb=hb, slot=slot):
                b.wait(t, "tensor", fills[s])
                b.wait(t, "tensor", commitB[s - 1] if hb == 0 else commitA[s])
                sl = b.slab[slot]
                for hh in range(8):
                    h = hb * 8 + hh
                    bank, col = pRp[hh // 4], (hh % 4) * 128
                    t.matmul(bank[:, col:col + 128],
                             sl[:, h * 256: h * 256 + 128],
                             b.mem16[:, h * 128:(h + 1) * 128],
                             start=True, stop=False)
                    ins = t.matmul(bank[:, col:col + 128], b.eyeb[:, :],
                                   sl[:, h * 256 + 128: h * 256 + 256],
                                   start=False, stop=True)
                b.inc(ins, "pe")
            hpe = b.cur("pe")
            if hb == 1:
                halfB_pe[s] = hpe

            @blk.vector
            def _(v, s=s, hb=hb, hpe=hpe, own=own):
                b.wait(v, "vector", hpe)
                off = hb * 1024
                if own:
                    for b2 in range(2):
                        ins = v.tensor_copy(
                            b.mem16b[:, off + b2 * 512: off + (b2 + 1) * 512],
                            pRp[b2][:, :])
                else:
                    v.tensor_scalar(b.keep2[:, off:off + 1024],
                                    b.mem16[:, off:off + 1024],
                                    b.rmask[:, NROUND + s:NROUND + s + 1],
                                    None, op0=ALU.mult)
                    for b2 in range(2):
                        ins = v.scalar_tensor_tensor(
                            b.mem16[:, off + b2 * 512: off + (b2 + 1) * 512],
                            pRp[b2][:, :], b.rmask[:, s:s + 1],
                            b.keep2[:, off + b2 * 512: off + (b2 + 1) * 512],
                            op0=ALU.mult, op1=ALU.add)
                tok = b.inc(ins, "dv")
                if hb == 0:
                    commitA[s] = tok
                else:
                    commitB[s] = tok
                    slab_cons[s % 2] = tok

        def att_dst(h, seg):
            return b.att_all.rearrange(
                "p (cc sg col) -> p cc sg col", cc=KD, sg=2)[:, :, seg,
                                                            h * 32:(h + 1) * 32]

        def pa_scores(i, jj):
            h, seg = i // 2, i % 2
            qsl = b.q_fm[:, h * T + seg * 512: h * T + (seg + 1) * 512]
            ksl = b.k_fm[:, h * T + seg * 512: h * T + (seg + 1) * 512]
            et = ET[i % 3]
            for j in (2 * jj, 2 * jj + 1):
                bk = (4 * i + j) % 3

                @blk.tensor
                def _(t, i=i, j=j, bk=bk, qsl=qsl, ksl=ksl):
                    b.wait(t, "tensor", self.k_tok)
                    b.wait(t, "tensor", self.qh_tok[(i // 2) // 4])
                    b.wait(t, "tensor", maskt_tok)
                    b.wait(t, "tensor", v_tok)
                    b.wait(t, "tensor", exp_done[bk])
                    t.matmul(pS[bk][:, :], ksl[:, j * 128:(j + 1) * 128], qsl,
                             start=True, stop=False)
                    ins = t.matmul(pS[bk][:, :], b.eyeb[:, :],
                                   b.maskt[:, j * 512:(j + 1) * 512],
                                   start=False, stop=True)
                    b.inc(ins, "pe")
                spe = b.cur("pe")

                @blk.scalar
                def _(sc, i=i, j=j, bk=bk, et=et, spe=spe):
                    b.wait(sc, "scalar", spe)
                    b.wait(sc, "scalar", b.prelude_io)
                    if i < 3:
                        b.wait(sc, "scalar", gb_all)
                        b.wait(sc, "scalar", gb_all2)
                        b.wait(sc, "scalar", spill_tok)
                    b.wait(sc, "scalar", expT_war.get(i - 3))
                    b.inc(sc.activation(et[:, j * 512:(j + 1) * 512],
                                        pS[bk][:, :], AF.Exp,
                                        bias=b.shiftc[:, 0:1], scale=SCALE),
                          "ac")
                exp_done[bk] = b.cur("ac")
            if jj == 1:
                sc_pe[i] = b.cur("pe")
                exp_tok[i] = b.cur("ac")

        def pa_sq(i):
            h, seg = i // 2, i % 2
            qsl = b.q_fm[:, h * T + seg * 512: h * T + (seg + 1) * 512]

            @blk.vector
            def _(v, i=i, qsl=qsl):
                b.wait(v, "vector", sqe_ac[i - 1])
                b.inc(v.tensor_scalar(b.fsc[:, 0:512], qsl, 0.0, None,
                                      op0=ALU.min), "dv")
            mn_dv = b.cur("dv")

            @blk.scalar
            def _(sc, mn_dv=mn_dv):
                b.wait(sc, "scalar", mn_dv)
                b.inc(sc.activation(b.fsc[:, 0:512], b.fsc[:, 0:512], AF.Exp),
                      "ac")
            sqe_ac[i] = b.cur("ac")

            @blk.vector
            def _(v, i=i, qsl=qsl):
                b.wait(v, "vector", sqe_ac[i])
                b.wait(v, "vector", sc_pe[i])
                v.tensor_scalar(b.fsc[:, 512:1024], qsl, 0.0, None,
                                op0=ALU.max)
                b.inc(v.tensor_tensor(qsl, b.fsc[:, 0:512],
                                      b.fsc[:, 512:1024], op=ALU.add), "dv")
            sq_done[i] = b.cur("dv")

        def pa_ad(i):
            h, seg = i // 2, i % 2
            et = ET[i % 3]

            @blk.tensor
            def _(t, i=i, h=h, seg=seg, et=et):
                b.wait(t, "tensor", exp_tok[i])
                b.wait(t, "tensor", v_tok)
                b.wait(t, "tensor", ev_dv[i - 1])
                for qb in range(4):
                    for j in range(4):
                        t.matmul(pAT[:, qb * 128:(qb + 1) * 128],
                                 et[:, j * 512 + qb * 128: j * 512 + (qb + 1) * 128],
                                 b.v_tm[:, (seg * 4 + j) * D + h * 128:
                                        (seg * 4 + j) * D + (h + 1) * 128],
                                 start=(j == 0), stop=(j == 3))
                    for j in range(4):
                        ins = t.matmul(pDT[:, qb:qb + 1],
                                 et[:, j * 512 + qb * 128: j * 512 + (qb + 1) * 128],
                                 b.o128[:, 0:1],
                                 start=(j == 0), stop=(j == 3))
                b.inc(ins, "pe")
            ad_pe[i] = b.cur("pe")
            expT_war[i] = ad_pe[i]

            @blk.vector
            def _(v, i=i):
                b.wait(v, "vector", ad_pe[i])
                b.wait(v, "vector", tr_pe.get(i - 2))
                tok = b.inc(v.reciprocal(b.rdc[:, 0:4], pDT[:, 0:4]), "dv")
                b.wait(v, "vector", tok)
                for qb in range(4):
                    ins = v.tensor_scalar(
                        b.attTs[i % 2][:, qb * 128:(qb + 1) * 128],
                        pAT[:, qb * 128:(qb + 1) * 128],
                        b.rdc[:, qb:qb + 1], None, op0=ALU.mult)
                b.inc(ins, "dv")
            ev_dv[i] = b.cur("dv")

        def pa_tr(i):
            h, seg = i // 2, i % 2

            @blk.tensor
            def _(t, i=i):
                b.wait(t, "tensor", ev_dv[i])
                b.wait(t, "tensor", bl_dv[i - 1])
                for qb in range(4):
                    ins = t.transpose(cpt[:, qb * 128:(qb + 1) * 128],
                                      b.attTs[i % 2][:, qb * 128:(qb + 1) * 128],
                                      b.eyeb[:, :])
                b.inc(ins, "pe")
            tr_pe[i] = b.cur("pe")

            @blk.vector
            def _(v, i=i, h=h, seg=seg):
                b.wait(v, "vector", tr_pe[i])
                v.tensor_copy(b.attC[:, :], cpt[:, :])
                src = b.attC.rearrange("p (j cc) -> p cc j", j=32)
                ins = v.tensor_scalar(att_dst(h, seg), src,
                                      b.beta1m[:, h:h + 1], None, op0=ALU.mult)
                b.inc(ins, "dv")
            bl_dv[i] = b.cur("dv")

        def phaseB(i):
            k = 2 * H + i
            h, seg = i // 2, i % 2
            sqsl = b.q_fm[:, h * T + seg * 512: h * T + (seg + 1) * 512]
            mem = b.mem16 if seg == 0 else b.mem16b

            @blk.tensor
            def _(t, i=i, k=k, h=h, seg=seg, sqsl=sqsl, mem=mem):
                b.wait(t, "tensor", commitB[NROUND])
                b.wait(t, "tensor", sq_done[i])
                b.wait(t, "tensor", ev_dv[k - 1])
                for qb in range(4):
                    t.matmul(pDT[:, qb:qb + 1],
                             sqsl[:, qb * 128:(qb + 1) * 128],
                             b.z16[:, seg * H + h: seg * H + h + 1],
                             start=True, stop=True)
                    ins = t.matmul(pAT[:, qb * 128:(qb + 1) * 128],
                             sqsl[:, qb * 128:(qb + 1) * 128],
                             mem[:, h * 128:(h + 1) * 128],
                             start=True, stop=True)
                b.inc(ins, "pe")
            ad_pe[k] = b.cur("pe")

            @blk.vector
            def _(v, k=k):
                b.wait(v, "vector", ad_pe[k])
                b.wait(v, "vector", tr_pe.get(k - 2))
                tok = b.inc(v.reciprocal(b.rdc[:, 0:4], pDT[:, 0:4]), "dv")
                b.wait(v, "vector", tok)
                for qb in range(4):
                    ins = v.tensor_scalar(
                        b.attTs[k % 2][:, qb * 128:(qb + 1) * 128],
                        pAT[:, qb * 128:(qb + 1) * 128],
                        b.rdc[:, qb:qb + 1], None, op0=ALU.mult)
                b.inc(ins, "dv")
            ev_dv[k] = b.cur("dv")

            @blk.tensor
            def _(t, k=k):
                b.wait(t, "tensor", ev_dv[k])
                b.wait(t, "tensor", bl_dv[k - 1])
                for qb in range(4):
                    ins = t.transpose(cpt[:, qb * 128:(qb + 1) * 128],
                                      b.attTs[k % 2][:, qb * 128:(qb + 1) * 128],
                                      b.eyeb[:, :])
                b.inc(ins, "pe")
            tr_pe[k] = b.cur("pe")

            @blk.vector
            def _(v, k=k, h=h, seg=seg):
                b.wait(v, "vector", tr_pe[k])
                v.tensor_copy(b.attC[:, :], cpt[:, :])
                src = b.attC.rearrange("p (j cc) -> p cc j", j=32)
                dst = att_dst(h, seg)
                ins = v.scalar_tensor_tensor(dst, src, b.beta[:, h:h + 1],
                                             dst, op0=ALU.mult, op1=ALU.add)
                b.inc(ins, "dv")
            bl_dv[k] = b.cur("dv")

        emit_fill(0)
        emit_fill(1)
        pa_scores(0, 0)
        pa_scores(0, 1)
        for i in range(2 * H):
            if i + 1 < 2 * H:
                pa_scores(i + 1, 0)
            pa_ad(i)
            if i + 1 < 2 * H:
                pa_scores(i + 1, 1)
            pa_tr(i)
            pa_sq(i)
            hx = i - 4
            if 0 <= hx < 2 * (NROUND + 1):
                r, hb = hx // 2, hx % 2
                replay_half(r, hb)
                if hb == 1 and r + 2 <= NROUND:
                    emit_fill(r + 2)
        for hx in range(2 * H - 4, 2 * (NROUND + 1)):
            replay_half(hx // 2, hx % 2)

        for i in range(2 * H):
            phaseB(i)

        combA_f = ad_pe[2 * H - 1]
        combB_h, combB_f = ad_pe[2 * H + 15], ad_pe[4 * H - 1]
        own_pe = halfB_pe[NROUND]
        wo_last = tr_pe[4 * H - 1]
        comb_tok = b.cur("dv")
        pes4.close()

        woq_tok = {}

        @blk.gpsimd
        def _(g):
            b.wait(g, "gpsimd", sc_pe[15])
            woq_tok[2] = b.inc(g.dma_start(
                out=b.wo_res[:, 2 * 8192:3 * 8192],
                in_=self.wo_d[:, 2 * 8192:3 * 8192]), "w2", 16)
            b.wait(g, "gpsimd", sc_pe[2 * H - 1])
            woq_tok[3] = b.inc(g.dma_start(
                out=b.wo_res[:, 3 * 8192:4 * 8192],
                in_=self.wo_d[:, 3 * 8192:4 * 8192]), "w3", 16)
            b.wait(g, "gpsimd", combB_h)
            woq_tok[0] = b.inc(g.dma_start(
                out=b.wo_res[:, 0 * 8192:1 * 8192],
                in_=self.wo_d[:, 0 * 8192:1 * 8192]), "w0", 16)
            b.wait(g, "gpsimd", combB_f)
            woq_tok[1] = b.inc(g.dma_start(
                out=b.wo_res[:, 1 * 8192:2 * 8192],
                in_=self.wo_d[:, 1 * 8192:2 * 8192]), "w1", 16)

        rrel_ch = {}

        @blk.sync
        def _(sy):
            b.wait(sy, "sync", combA_f)
            b.wait(sy, "sync", own_pe)
            for i in (0, 1, 2, 3, 6):
                rrel_ch[i] = b.inc(
                    sy.dma_start(out=b.r[:, i * D:(i + 1) * D],
                                 in_=b.rsp_d[i * 128:(i + 1) * 128, :]),
                    "s0", 16)
            b.wait(sy, "sync", wo_last)
            b.wait(sy, "sync", comb_tok)
            rrel_ch[7] = b.inc(
                sy.dma_start(out=b.r[:, 7 * D:8 * D],
                             in_=b.rsp_d[7 * 128:8 * 128, :]), "s0", 16)

        @blk.scalar
        def _(sc):
            b.wait(sc, "scalar", combB_f)
            b.wait(sc, "scalar", comb_tok)
            for i in (4, 5):
                rrel_ch[i] = b.inc(
                    sc.dma_start(out=b.r[:, i * D:(i + 1) * D],
                                 in_=b.rsp_d[i * 128:(i + 1) * 128, :]),
                    "s1", 16)

        pes5 = ExitStack()
        pb2 = [pes5.enter_context(nc.psum_tensor(f"wpb{j}", [128, 512], F32))
               for j in range(8)]
        pb2_free = {j: None for j in range(8)}
        kk_order = list(range(8, 16)) + list(range(8))

        @blk.tensor
        def _(t):
            b.wait(t, "tensor", comb_tok)
            for ts in range(NT):
                grp = 4 * (ts % 2)
                for dsl in range(4):
                    b.wait(t, "tensor", pb2_free[grp + dsl])
                for kx, kk in enumerate(kk_order):
                    b.wait(t, "tensor", woq_tok.get(kk // 4))
                    for dsl in range(4):
                        ins = t.matmul(pb2[grp + dsl][:, :],
                                 b.att_all[:, kk * T + ts * 128: kk * T + (ts + 1) * 128],
                                 b.wo_res[:, kk * D + dsl * 512: kk * D + (dsl + 1) * 512],
                                 start=(kx == 0), stop=(kx == KD - 1))
                tok = b.inc(ins, "pe")

                @blk.vector
                def _(v, ts=ts, tok=tok, grp=grp):
                    b.wait(v, "vector", tok)
                    b.wait(v, "vector", rrel_ch[ts])
                    for dsl in range(4):
                        ins = v.tensor_tensor(
                            b.r[:, ts * D + dsl * 512: ts * D + (dsl + 1) * 512],
                            b.r[:, ts * D + dsl * 512: ts * D + (dsl + 1) * 512],
                            pb2[grp + dsl][:, :], op=ALU.add)
                    etok = b.inc(ins, "dv")
                    for dsl in range(4):
                        pb2_free[grp + dsl] = etok
        pes5.close()

    def emit_store(self):
        b, blk = self, self.blk
        fin = b.cur("dv")

        @blk.sync
        def _(sy):
            b.wait(sy, "sync", fin)
            for i in range(NT):
                b.inc(sy.dma_start(out=b.out_d[i * 128:(i + 1) * 128, :],
                                   in_=b.r[:, i * D:(i + 1) * D]), "st", 16)
            sy.wait_ge(b.sem_h["st"], b.sem_n["st"])


def _consts():
    sel8 = np.zeros((8, 8 * 128), dtype=ml_dtypes.bfloat16)
    for e in range(8):
        sel8[e, e * 128:(e + 1) * 128] = 1.0
    eyeb = np.eye(128, dtype=ml_dtypes.bfloat16)
    eyef = np.eye(128, dtype=np.float32)
    o128 = np.ones((128, 1), dtype=ml_dtypes.bfloat16)
    o1 = np.ones((1, 128), dtype=ml_dtypes.bfloat16)
    mt = np.zeros((128, 4 * SEG), dtype=np.float32)
    for j in range(4):
        kt = j * 128 + np.arange(128)[:, None]
        qt = np.arange(SEG)[None, :]
        mt[:, j * SEG:(j + 1) * SEG] = np.where(kt <= qt, 0.0, -1e9 / SCALE)
    return eyeb, eyef, o128, o1, mt.astype(ml_dtypes.bfloat16), sel8


def _core_masks(c):
    zm = np.zeros((128, 2 * NSEG), dtype=np.float32)
    for j in range(NSEG):
        zm[:, j] = 1.0 if j < 2 * c else 0.0
        zm[:, NSEG + j] = 1.0 if j < 2 * c + 1 else 0.0
    rm = np.zeros((128, 2 * NROUND), dtype=np.float32)
    for s in range(NROUND):
        m = 1.0 if s < 2 * c else 0.0
        rm[:, s] = m
        rm[:, NROUND + s] = 1.0 - m
    return zm, rm


def kernel(L0_attn_x, L0_attn_Wq, L0_attn_Wk, L0_attn_Wv, L0_attn_Wo,
           L0_attn_betas, L0_ffn_rms_weight, L0_ffn_router_w, L0_ffn_w_gate,
           L0_ffn_w_up, L0_ffn_w_down, L0_ffn_sw_gate, L0_ffn_sw_up,
           L0_ffn_sw_down, _parts=("ffn1", "attn", "ffn2")):
    x = np.ascontiguousarray(np.asarray(L0_attn_x, dtype=np.float32).reshape(S, D))
    eyeb, eyef, o128, o1, mt, sel8 = _consts()
    def _pm_qkv(w):
        return np.ascontiguousarray(
            np.asarray(w, np.float32).reshape(KD, 128, 4, 512)
            .transpose(1, 2, 0, 3).reshape(128, 4 * KD * 512))

    def _pm_half(w):
        return np.ascontiguousarray(
            np.asarray(w, np.float32).reshape(KD, 128, 2, 512)
            .transpose(1, 2, 0, 3).reshape(128, 2 * KD * 512))

    def _pm_down(w):
        return np.ascontiguousarray(
            np.asarray(w, np.float32).reshape(FC, 128, D)
            .transpose(1, 0, 2).reshape(128, FC * D))

    def _pm_wo(w):
        return np.ascontiguousarray(
            np.asarray(w, np.float32).reshape(KD, 128, D)
            .transpose(1, 0, 2).reshape(128, KD * D))

    common = {
        "wq": _pm_qkv(L0_attn_Wq),
        "wk": _pm_qkv(L0_attn_Wk),
        "wv": _pm_qkv(L0_attn_Wv),
        "wo": _pm_wo(L0_attn_Wo),
        "betas": np.ascontiguousarray(np.asarray(L0_attn_betas, np.float32).reshape(H, DV)),
        "rmsw": np.ascontiguousarray(np.asarray(L0_ffn_rms_weight, np.float32).reshape(KD, 128).T),
        "router": np.ascontiguousarray(
            np.asarray(L0_ffn_router_w, np.float32).reshape(KD, 128, E)
            .transpose(1, 0, 2).reshape(128, KD * E)),
        "wg": np.stack([_pm_half(L0_ffn_w_gate[e]) for e in range(E)]),
        "wu": np.stack([_pm_half(L0_ffn_w_up[e]) for e in range(E)]),
        "wd": np.stack([_pm_down(L0_ffn_w_down[e]) for e in range(E)]),
        "swg": _pm_half(L0_ffn_sw_gate),
        "swu": _pm_half(L0_ffn_sw_up),
        "swd": _pm_down(L0_ffn_sw_down),
        "eyeb": eyeb, "eyef": eyef, "o128": o128, "o1": o1, "maskt": mt,
        "sel8": sel8,
        "shiftc": np.full((128, 1), SHIFT, dtype=np.float32),
    }
    in_maps = []
    for c in range(NCORES):
        zm, rm = _core_masks(c)
        m = dict(common)
        m["x"] = np.ascontiguousarray(x[c * T:(c + 1) * T])
        m["zmaskc"] = zm
        m["rmaskc"] = rm
        in_maps.append(m)

    bld = B()
    nc = bld.build(parts=_parts)
    res = run_bass_kernel_spmd(nc, in_maps, core_ids=list(range(NCORES)))
    out = np.concatenate([np.asarray(r["out"], dtype=np.float32)
                          for r in res.results], axis=0)
    return out.reshape(1, S, D)
